# version 11
# baseline (speedup 1.0000x reference)
import os
import numpy as np
import ml_dtypes
import concourse.bass as bass
import concourse.mybir as mybir
from concourse.bass import IndirectOffsetOnAxis
from concourse.bass_utils import run_bass_kernel_spmd

F32 = mybir.dt.float32
BF16 = mybir.dt.bfloat16
U8 = mybir.dt.uint8
AF = mybir.ActivationFunctionType
ALU = mybir.AluOpType
AX = mybir.AxisListType.X
BF = ml_dtypes.bfloat16

D = 2048
H = 16
HD = 128
CH = 128
TPC = 1024
NT = TPC // 128
NG = TPC // 256
NE = 16
EI = 1408
NJ = EI // 128
SI = 2816
NSJ = SI // 128
EPS = 1e-5
CCAP = 256
PADROW = NE * CCAP

_slopes = (2.0 ** (-0.5)) ** (np.arange(1, H + 1, dtype=np.float64))
LAM = np.exp(-_slopes.astype(np.float32)).astype(np.float32)
LAMC = (LAM.astype(np.float64) ** CH).astype(np.float32)


def _host_consts():
    t = np.arange(CH, dtype=np.float64)
    lam = LAM.astype(np.float64)
    masks = np.zeros((128, H * 128), np.float64)
    dqb = np.zeros((128, H * 128), np.float64)
    deck = np.zeros((128, H), np.float64)
    for h in range(H):
        diff = t[None, :] - t[:, None]
        m = np.where(diff >= 0, lam[h] ** np.maximum(diff, 0), 0.0)
        masks[:, h * 128:(h + 1) * 128] = m
        dqb[:, h * 128:(h + 1) * 128] = np.broadcast_to(lam[h] ** (t + 1.0), (128, 128))
        deck[:, h] = lam[h] ** (127.0 - t)
    iota = np.broadcast_to(np.arange(16, dtype=np.float32), (128, 16)).copy()
    ub = np.zeros((8, 128, 1024), np.float64)
    for T in range(8):
        tp = T * 128 + np.arange(128)
        ub[T] = (tp[:, None] <= np.arange(1024)[None, :]).astype(np.float64)
    return {
        "ubig": ub.astype(BF),
        "masks": masks.astype(BF),
        "dqb": dqb.astype(BF),
        "deck": deck.astype(np.float32),
        "identf": np.eye(128, dtype=np.float32),
        "identb": np.eye(128, dtype=BF),
        "ones1": np.ones((1, 128), np.float32),
        "zeros": np.zeros((128, 2048), BF),
        "iota": iota,
        "iotaB": (iota + 1e6).astype(np.float32),
    }


def _images(L0_attn_w_qkv, L0_attn_b_qkv, L0_attn_w_out, L0_attn_b_out,
            L0_ffn_gate_w, L0_ffn_w1, L0_ffn_w2, L0_ffn_w3,
            L0_ffn_sw1, L0_ffn_sw2, L0_ffn_sw3):
    asc = np.ascontiguousarray
    wqkv = asc(np.asarray(L0_attn_w_qkv, np.float32).reshape(48, 128, 16, 128)
               .transpose(0, 3, 2, 1).reshape(48, 128, 2048))
    bqkv = asc(np.asarray(L0_attn_b_qkv, np.float32).reshape(48, 128).T)
    wout = asc(np.asarray(L0_attn_w_out, np.float32).T.reshape(16, 128, 2048))
    boutr = asc(np.asarray(L0_attn_b_out, np.float32).reshape(1, 2048))
    gateb = asc(np.asarray(L0_ffn_gate_w, np.float32).reshape(16, 16, 128)
                .transpose(2, 1, 0).reshape(128, 256))
    w1 = np.asarray(L0_ffn_w1)
    w3 = np.asarray(L0_ffn_w3)
    w2 = np.asarray(L0_ffn_w2)
    sw1 = np.asarray(L0_ffn_sw1)
    sw3 = np.asarray(L0_ffn_sw3)
    sw2 = np.asarray(L0_ffn_sw2)
    w1i_ = w1.reshape(16, 11, 128, 16, 128).transpose(0, 1, 4, 3, 2).reshape(16, 11, 128, 2048)
    w3i_ = w3.reshape(16, 11, 128, 16, 128).transpose(0, 1, 4, 3, 2).reshape(16, 11, 128, 2048)
    w13i = asc(np.concatenate([w1i_, w3i_], axis=-1))
    w2i = asc(w2.reshape(16, 4, 512, 11, 128).transpose(0, 1, 4, 3, 2)
              .reshape(16, 4, 128, 5632))
    sw1i_ = sw1.reshape(22, 128, 16, 128).transpose(0, 3, 2, 1).reshape(22, 128, 2048)
    sw3i_ = sw3.reshape(22, 128, 16, 128).transpose(0, 3, 2, 1).reshape(22, 128, 2048)
    sw13i = asc(np.concatenate([sw1i_, sw3i_], axis=-1))
    sw2i = asc(sw2.reshape(4, 512, 2, 11, 128).transpose(2, 0, 4, 3, 1)
               .reshape(2, 4, 128, 5632))
    return dict(wqkv=wqkv, bqkv=bqkv, wout=wout, boutr=boutr, gateb=gateb,
                w13i=w13i, w2i=w2i, sw13i=sw13i, sw2i=sw2i)


SEMS = ["dc", "dxx", "dwa", "dwb", "dw2a", "dw2b", "dga", "dgb", "do", "pe", "dve", "act",
        "dua", "dub", "dxw", "dxb", "dsc", "dxe", "dye", "dgt", "dz", "dk", "dgc", "dwa2", "dwb2", "dxx2"]


class S:

    def __init__(self, kind, eng, sems):
        self.kind = kind
        self.e = eng
        self.sems = sems
        self.n = {k: 0 for k in SEMS}
        self.waited = {k: 0 for k in SEMS}

    def emit(self, ek, fn=None, waits=(), inc=None, inc_n=1):
        for s, v in waits:
            assert v <= self.n[s], f"wait-on-future {s}: {v} > {self.n[s]}"
        if self.kind == ek:
            for s, v in waits:
                if v > self.waited[s]:
                    self.e.wait_ge(self.sems[s], v)
                    self.waited[s] = v
            ins = fn(self.e) if fn is not None else None
            if inc is not None and ins is not None:
                ins.then_inc(self.sems[inc], inc_n)
        if inc is not None:
            self.n[inc] += inc_n
        return self.n[inc] if inc is not None else None


PHASES = int(os.environ.get("K_PHASES", "4"))
KSUB = int(os.environ.get("K_SUB", "9"))
KLN = int(os.environ.get("K_LN", "9"))
KP3 = int(os.environ.get("K_P3", "9"))


def build_nc():
    nc = bass.Bass("TRN2", target_bir_lowering=False)
    dt = nc.dram_tensor
    xo = dt("xo", [TPC, D], F32, kind="ExternalInput").ap()
    xp = dt("xp", [TPC, D], F32, kind="ExternalInput").ap()
    gatev_h = dt("gatev", [128, 1], F32, kind="ExternalInput").ap()
    wqkv_h = dt("wqkv", [48, 128, 2048], F32, kind="ExternalInput").ap()
    bqkv_h = dt("bqkv", [128, 48], F32, kind="ExternalInput").ap()
    wout_h = dt("wout", [16, 128, 2048], F32, kind="ExternalInput").ap()
    boutr_h = dt("boutr", [1, 2048], F32, kind="ExternalInput").ap()
    gateb_h = dt("gateb", [128, 256], F32, kind="ExternalInput").ap()
    w13_h = dt("w13i", [16, 11, 128, 4096], BF16, kind="ExternalInput").ap()
    w2_h = dt("w2i", [16, 4, 128, 5632], BF16, kind="ExternalInput").ap()
    sw13_h = dt("sw13i", [22, 128, 4096], BF16, kind="ExternalInput").ap()
    sw2_h = dt("sw2i", [2, 4, 128, 5632], BF16, kind="ExternalInput").ap()
    masks_h = dt("masks", [128, 2048], BF16, kind="ExternalInput").ap()
    dqb_h = dt("dqb", [128, 2048], BF16, kind="ExternalInput").ap()
    deck_h = dt("deck", [128, 16], F32, kind="ExternalInput").ap()
    identf_h = dt("identf", [128, 128], F32, kind="ExternalInput").ap()
    identb_h = dt("identb", [128, 128], BF16, kind="ExternalInput").ap()
    ones1_h = dt("ones1", [1, 128], F32, kind="ExternalInput").ap()
    iota_h = dt("iota", [128, 16], F32, kind="ExternalInput").ap()
    iotaB_h = dt("iotaB", [128, 16], F32, kind="ExternalInput").ap()
    ubig_h = dt("ubig", [8, 128, 1024], BF16, kind="ExternalInput").ap()
    zeros_h = dt("zeros", [128, 2048], BF16, kind="ExternalInput").ap()
    y_h = dt("y", [TPC, D], F32, kind="ExternalOutput").ap()
    xe_d = dt("xe_d", [PADROW + 128, D], BF16, kind="Internal").ap()
    wqkv16_d = dt("wqkv16_d", [48, 128, 2048], BF16, kind="Internal").ap()
    wout16_d = dt("wout16_d", [16, 128, 2048], BF16, kind="Internal").ap()
    ye_d = dt("ye_d", [PADROW + 128, D], BF16, kind="Internal").ap()
    xntm_d = dt("xntm_d", [TPC, D], BF16, kind="Internal").ap()

    alloc = nc.alloc_sbuf_tensor
    xn_fm = alloc("xn_fm", [128, 16 * TPC], BF16).ap()
    out_acc = alloc("out_acc", [128, NT * D], F32).ap()
    bout_bc = alloc("bout_bc", [128, D], BF16).ap()
    masks = alloc("masksb", [128, 2048], BF16).ap()
    dqb = alloc("dqbb", [128, 2048], BF16).ap()
    Ssb = alloc("Ssb", [128, H * 128], F32).ap()
    deck = alloc("deckb", [128, 16], F32).ap()
    identf = alloc("identfb", [128, 128], F32).ap()
    identb = alloc("identbb", [128, 128], BF16).ap()
    ones1 = alloc("ones1b", [1, 128], F32).ap()
    bqkv = alloc("bqkvb", [128, 48], F32).ap()
    gateb = alloc("gatebb", [128, 256], F32).ap()
    iota = alloc("iotab", [128, 16], F32).ap()
    iotaB = alloc("iotaBb", [128, 16], F32).ap()
    boutr = alloc("boutrb", [1, 2048], F32).ap()
    gatev = alloc("gatevb", [128, 1], F32).ap()
    gs = alloc("gsb", [128, NT * 16], F32).ap()
    dw = alloc("dwb_tm", [128, NT * 16], F32).ap()
    lnc = alloc("lncol", [128, 16], F32).ap()
    tks = alloc("tkscratch", [128, 64], F32).ap()
    rank_tm = alloc("rank_tm", [128, 128], F32).ap()
    i12 = alloc("i12b", [128, 16], F32).ap()
    wcols = alloc("wcolsb", [128, 16], F32).ap()
    idxc = alloc("idxcb", [128, 16], mybir.dt.int32).ap()

    AR_BYTES = 74240
    arena = alloc("arena", [128, AR_BYTES], U8)

    def av(off, cols, dtp, rows=128):
        bw = 4 if dtp == F32 else (2 if dtp == BF16 else 1)
        ap_ = arena.ap()[0:rows, off:off + cols * bw].bitcast(dtp)
        return ap_, off + cols * bw

    o = 0
    xb_t, o = av(o, 2048, F32)
    xnt, o = av(o, 2048, F32)
    scr2, o = av(o, 256, F32)
    qb_off = o
    qb, o = av(o, 4096, BF16)
    kb, o = av(o, 4096, BF16)
    vb, o = av(o, 4096, BF16)
    xnp, o = av(o, 4096, BF16)
    wqs_off = o
    wqs0, o = av(o, 2048, BF16)
    wqs1, o = av(o, 2048, BF16)
    attn_g_off = o
    attn_g, o = av(o, 4096, BF16)
    kdT0, o = av(o, 128, BF16)
    kdT1, o = av(o, 128, BF16)
    vT0, o = av(o, 128, BF16)
    vT1, o = av(o, 128, BF16)
    qd0, o = av(o, 128, BF16)
    qd1, o = av(o, 128, BF16)
    sm0_off = o
    sm0, o = av(o, 128, BF16)
    sm1, o = av(o, 128, BF16)
    sbf0, o = av(o, 128, BF16)
    sbf1, o = av(o, 128, BF16)
    xnb_off = o
    xnb, o = av(o, 2048, BF16)
    assert o <= AR_BYTES, o
    oq = attn_g_off
    uslab = []
    for k in range(2):
        w, oq = av(oq, 1024, BF16)
        uslab.append(w)
    rankf, _ = av(oq, 1024, F32, rows=16)
    maskb, _ = av(sm0_off, 128, BF16)
    wqs2, _ = av(xnb_off, 2048, BF16)
    xnt2, _ = av(attn_g_off, 2048, F32)
    xbP = [xb_t, xb_t]
    xntP = [xnt, xnt2]
    o = 0
    hbuf, o = av(o, NJ * TPC, BF16)
    w13 = []
    w13P = []
    for k in range(4):
        if k % 2 == 0:
            w13P.append(arena.ap()[0:128, o:o + 8192].bitcast(BF16))
        w, o = av(o, 2048, BF16)
        w13.append(w)
    w2g0, o = av(o, 5632, BF16)
    w2g1, o = av(o, 5632, BF16)
    silu0, o = av(o, 512, BF16)
    silu1, o = av(o, 512, BF16)
    assert o <= AR_BYTES, o
    os_ = 0
    xe_tm, os_ = av(os_, 2 * 2048, BF16)
    xe_fm, os_ = av(os_, 4096, BF16)
    he, os_ = av(os_, NJ * 256, BF16)
    assert os_ <= 22528, os_
    ye_sb, o = av(o, 2 * 2048, BF16)
    assert o <= AR_BYTES, o
    ybuf = [xe_tm[:, 0:2048], xe_tm[:, 2048:4096]]
    zb16 = xb_t.bitcast(BF16)

    kdT = [kdT0, kdT1]
    vT = [vT0, vT1]
    qd = [qd0, qd1]
    sm = [sm0, sm1]
    sbf = [sbf0, sbf1]
    wqs = [wqs0, wqs1, wqs2]
    w2g = [w2g0, w2g1]
    silu_sb = [silu0, silu1]

    pb = [nc.alloc_psum_tensor(f"pb{i}", [128, 512], F32).ap() for i in range(8)]

    def pbf(i, cols=128):
        return pb[i][:, 0:cols]

    def pbbf(i):
        return pb[i][:, 0:64].bitcast(BF16)

    sems = {}

    def sched(c: S):
        def dma(queue, sem, dst, src, waits=()):
            return c.emit(queue, lambda e: e.dma_start(dst, src), waits, sem, 16)
        pe = lambda fn, waits=(), inc=True: c.emit("tensor", fn, waits, "pe" if inc else None)
        dv = lambda fn, waits=(), inc=True: c.emit("vector", fn, waits, "dve" if inc else None)
        ac = lambda fn, waits=(), inc=True: c.emit("scalar", fn, waits, "act" if inc else None)

        for dst, src in [(masks, masks_h), (dqb, dqb_h), (deck, deck_h),
                         (identf, identf_h), (identb, identb_h), (ones1, ones1_h),
                         (bqkv, bqkv_h), (gateb, gateb_h), (iota, iota_h),
                         (iotaB, iotaB_h), (boutr, boutr_h), (gatev, gatev_h)]:
            dma("sync", "dc", dst, src)
        consts = c.n["dc"]
        for i in range(33):
            dma("gpsimd", "dz", xe_d[i * 128:(i + 1) * 128, :], zeros_h,
                waits=[("dc", consts)])
        dma("gpsimd", "dz", ye_d[PADROW:PADROW + 128, :], zeros_h,
            waits=[("dc", consts)])
        dz_total = c.n["dz"]
        attn_g_zwait = [[("dz", dz_total)]]
        for jc in list(range(16, 48)) + list(range(0, 16)):
            dma("gpsimd", "dk", wqkv16_d[jc], wqkv_h[jc])
        wq16_done = c.n["dk"]
        for kc in range(16):
            dma("gpsimd", "dk", wout16_d[kc], wout_h[kc])
        wo16_done = c.n["dk"]
        for dcol in range(4):
            pe(lambda e, d=dcol: e.matmul(pb[3][:, 0:512], ones1, boutr[:, d * 512:(d + 1) * 512],
                                          start=True, stop=True, skip_group_check=True),
               waits=[("dc", consts), ("act", c.n["act"])])
            ac(lambda e, d=dcol: e.activation(bout_bc[:, d * 512:(d + 1) * 512], pb[3][:, 0:512], AF.Copy),
               waits=[("pe", c.n["pe"])])
        bout_ready = c.n["act"]
        S_zero = dv(lambda e: e.memset(Ssb, 0.0))

        def ln_tile(x_src, tile_guard_waits, do_gate, dst_fm_col_of, gate_tt,
                    mid_cb=None, xb_t=xb_t, xnt=xnt, xsem="dxx"):
            dma("sync", xsem, xb_t, x_src, waits=tile_guard_waits)
            ld = c.n[xsem]
            r0 = dv(lambda e: e.reduce_sum(lnc[:, 0:1], xb_t, axis=AX), waits=[(xsem, ld)])
            a0 = ac(lambda e: e.activation(xnt, xb_t, AF.Square),
                    waits=[(xsem, ld), ("pe", c.n["pe"])])
            if mid_cb is not None:
                mid_cb()
            r3 = dv(lambda e: e.reduce_sum(lnc[:, 2:3], xnt, axis=AX),
                    waits=[("act", a0)])
            r1 = dv(lambda e: e.tensor_scalar(lnc[:, 1:2], lnc[:, 0:1], 1.0 / D, None, ALU.mult),
                    waits=[("dve", r3)])
            r5 = dv(lambda e: e.tensor_mul(lnc[:, 5:6], lnc[:, 1:2], lnc[:, 1:2]), waits=[("dve", r1)])
            r5 = dv(lambda e: e.tensor_scalar(lnc[:, 7:8], lnc[:, 2:3], 1.0 / D, EPS, ALU.mult, ALU.add),
                    waits=[("dve", r5)])
            r5 = dv(lambda e: e.tensor_sub(lnc[:, 6:7], lnc[:, 7:8], lnc[:, 5:6]), waits=[("dve", r5)])
            a1 = ac(lambda e: e.activation(lnc[:, 3:4], lnc[:, 6:7], AF.Sqrt),
                    waits=[("dve", r5)])
            r4 = dv(lambda e: e.reciprocal(lnc[:, 4:5], lnc[:, 3:4]), waits=[("act", a1)])
            r6 = dv(lambda e: e.tensor_mul(lnc[:, 8:9], lnc[:, 4:5], lnc[:, 4:5]), waits=[("dve", r4)])
            r7 = dv(lambda e: e.tensor_mul(lnc[:, 9:10], lnc[:, 6:7], lnc[:, 8:9]), waits=[("dve", r6)])
            r8 = dv(lambda e: e.tensor_scalar(lnc[:, 10:11], lnc[:, 9:10], -0.5, 1.5, ALU.mult, ALU.add),
                    waits=[("dve", r7)])
            r9 = dv(lambda e: e.tensor_mul(lnc[:, 11:12], lnc[:, 4:5], lnc[:, 10:11]), waits=[("dve", r8)])
            rA = dv(lambda e: e.tensor_scalar(xnt, xb_t, lnc[:, 1:2], lnc[:, 11:12],
                                              ALU.subtract, ALU.mult),
                    waits=[("dve", r9)])
            if do_gate:
                axn = ac(lambda e: e.activation(xnb, xnt, AF.Copy),
                         waits=[("dve", rA), ("dxw", c.n["dxw"])])
                dma("sync", "dxw", xntm_d[gate_tt * 128:(gate_tt + 1) * 128, :], xnb,
                    waits=[("act", axn)])
            for dc in range(16 if KLN >= 2 else 0):
                par = dc % 2
                bank = par
                tw = [("dve", rA), ("act", c.n["act"] if dc < 2 else tr_act[par]),
                      ("dve", c.n["dve"] if (not (do_gate and KLN >= 3) or dc < 2) else tr_dve[par])]
                t_pe = pe(lambda e, d=dc, b=bank: e.transpose(pbf(b), xnt[:, d * 128:(d + 1) * 128], identf),
                          waits=tw)
                if do_gate and KLN >= 3:
                    tr_dve[par] = dv(lambda e, b=bank, p=par: e.tensor_copy(scr2[:, p * 128:(p + 1) * 128], pbf(b)),
                                     waits=[("pe", t_pe), ("pe", gate_pe[par])])
                tr_act[par] = ac(lambda e, b=bank, col=dst_fm_col_of(dc): e.activation(
                    xn_fm[:, col:col + 128] if do_gate else xnp[:, col:col + 128], pbf(b), AF.Copy),
                    waits=[("pe", t_pe)] + ([("dve", tr_dve[par])] if (do_gate and KLN >= 3) else []))
                if do_gate and KLN >= 4:
                    gate_pe[par] = pe(lambda e, d=dc, p=par: e.matmul(
                        pb[2][:, 0:16], scr2[:, p * 128:(p + 1) * 128], gateb[:, d * 16:(d + 1) * 16],
                        start=(d == 0), stop=(d == 15), skip_group_check=True),
                        waits=[("dve", tr_dve[par])] + ([("act", gate_bank_guard[0])] if dc == 0 else []))
            if do_gate and KLN >= 5:
                gate_bank_guard[0] = ac(lambda e, t=gate_tt: e.activation(
                    gs[:, t * 16:(t + 1) * 16], pb[2][:, 0:16], AF.Sigmoid),
                    waits=[("pe", gate_pe[1])])
            return rA

        tr_act = [0, 0]
        tr_dve = [0, 0]
        gate_pe = [0, 0]
        gate_bank_guard = [0]

        out_init = []
        xbg = [[("dc", consts)], [("dc", consts)]]
        for tt in range(NT if KSUB >= 2 else 0):
            p01 = tt % 2
            def _mid(t=tt, p=p01):
                oi = dv(lambda e: e.tensor_tensor(
                    out_acc[:, t * D:(t + 1) * D], xbP[p], bout_bc, ALU.add),
                    waits=[("act", bout_ready)])
                out_init.append(oi)
            ln_tile(xo[tt * 128:(tt + 1) * 128, :], xbg[p01], True,
                    (lambda d, t=tt: d * TPC + t * 128), tt, mid_cb=_mid,
                    xb_t=xbP[p01], xnt=xntP[p01], xsem="dxx")
            xbg[p01] = [("dve", c.n["dve"]), ("act", c.n["act"]), ("pe", c.n["pe"])]
        xn_main_done_act = c.n["act"]
        xb_guard = [("dve", c.n["dve"]), ("act", c.n["act"]), ("pe", c.n["pe"])]

        route = {"dw_done": 0, "scat_total": 0, "rf": [0, 0], "rt_act": 0}

        def emit_dw_routing():
          for tt in range(NT if KSUB >= 3 else 0):
              g0 = gs[:, tt * 16:(tt + 1) * 16]
              d0 = dw[:, tt * 16:(tt + 1) * 16]
              m1, m2 = lnc[:, 10:11], lnc[:, 11:12]
              i1, i2 = lnc[:, 12:13], lnc[:, 13:14]
              w1c, w2c = lnc[:, 14:15], lnc[:, 15:16]
              eq = tks[:, 0:16]
              tmp = tks[:, 16:32]
              s2 = tks[:, 32:48]
              oh1 = tks[:, 48:64]
              k0 = dv(lambda e, g=g0: e.reduce_max(m1, g, axis=AX), waits=[("act", gate_bank_guard[0])])
              k0 = dv(lambda e, g=g0: e.tensor_single_scalar(eq, g, m1, ALU.is_equal), waits=[("dve", k0)])
              k0 = dv(lambda e: e.scalar_tensor_tensor(tmp, eq, -1e6, iotaB, ALU.mult, ALU.add), waits=[("dve", k0)])
              k0 = dv(lambda e: e.tensor_reduce(i1, tmp, axis=AX, op=ALU.min), waits=[("dve", k0)])
              k0 = dv(lambda e: e.tensor_single_scalar(oh1, iota, i1, ALU.is_equal), waits=[("dve", k0)])
              k0 = dv(lambda e, g=g0: e.scalar_tensor_tensor(s2, oh1, -1e6, g, ALU.mult, ALU.add), waits=[("dve", k0)])
              k0 = dv(lambda e: e.reduce_max(m2, s2, axis=AX), waits=[("dve", k0)])
              k0 = dv(lambda e: e.tensor_single_scalar(eq, s2, m2, ALU.is_equal), waits=[("dve", k0)])
              k0 = dv(lambda e: e.scalar_tensor_tensor(tmp, eq, -1e6, iotaB, ALU.mult, ALU.add), waits=[("dve", k0)])
              k0 = dv(lambda e: e.tensor_reduce(i2, tmp, axis=AX, op=ALU.min), waits=[("dve", k0)])
              k0 = dv(lambda e: e.tensor_add(w1c, m1, m2), waits=[("dve", k0)])
              k0 = dv(lambda e: e.reciprocal(w2c, w1c), waits=[("dve", k0)])
              k0 = dv(lambda e: e.tensor_mul(w1c, m1, w2c), waits=[("dve", k0)])
              k0 = dv(lambda e: e.tensor_mul(w2c, m2, w2c), waits=[("dve", k0)])
              k0 = dv(lambda e: e.tensor_single_scalar(oh1, iota, i1, ALU.is_equal), waits=[("dve", k0)])
              k0 = dv(lambda e: e.tensor_single_scalar(eq, iota, i2, ALU.is_equal), waits=[("dve", k0)])
              k0 = dv(lambda e: e.tensor_single_scalar(s2, oh1, w1c, ALU.mult), waits=[("dve", k0)])
              k0 = dv(lambda e, d=d0: e.scalar_tensor_tensor(d, eq, w2c, s2, ALU.mult, ALU.add), waits=[("dve", k0)])
              k0 = dv(lambda e, t=tt: e.tensor_copy(i12[:, 2 * t:2 * t + 1], i1), waits=[("dve", k0)])
              k0 = dv(lambda e, t=tt: e.tensor_copy(i12[:, 2 * t + 1:2 * t + 2], i2), waits=[("dve", k0)])
              k0 = dv(lambda e, t=tt: e.tensor_copy(wcols[:, 2 * t:2 * t + 1], w1c), waits=[("dve", k0)])
              k0 = dv(lambda e, t=tt: e.tensor_copy(wcols[:, 2 * t + 1:2 * t + 2], w2c), waits=[("dve", k0)])
          route["dw_done"] = c.n["dve"]

          if KSUB >= 3:
              mprev = 0
              for tt in range(NT):
                  mprev = dv(lambda e, t=tt: e.tensor_scalar(
                      maskb[:, t * 16:(t + 1) * 16], dw[:, t * 16:(t + 1) * 16],
                      0.0, None, ALU.is_gt), waits=[("dve", route["dw_done"])])
              mask_done = mprev
              uprev = [c.n["pe"], c.n["pe"]]
              rk_stop = [0, 0]
              for T in range(NT):
                  par = T % 2
                  dus = "dua" if par == 0 else "dub"
                  dma("sync", dus, uslab[par], ubig_h[T], waits=[("pe", uprev[par])])
                  ldu = c.n[dus]
                  for half in range(2):
                      w = [(dus, ldu), ("dve", mask_done)] if half == 0 else []
                      mm = pe(lambda e, t=T, hh=half: e.matmul(
                          pb[4 + hh][0:16, 0:512], maskb[:, t * 16:(t + 1) * 16],
                          uslab[t % 2][:, hh * 512:(hh + 1) * 512],
                          start=(t == 0), stop=(t == NT - 1), skip_group_check=True),
                          waits=w, inc=True)
                      if T == NT - 1:
                          rk_stop[half] = mm
                  uprev[par] = c.n["pe"]
              rf = [0, 0]
              for half in range(2):
                  rf[half] = dv(lambda e, hh=half: e.tensor_copy(
                      rankf[:, hh * 512:(hh + 1) * 512], pb[4 + hh][0:16, 0:512]),
                      waits=[("pe", rk_stop[half])])
              rt_prev = 0
              for T in range(NT):
                  par = T % 2
                  t_pe = pe(lambda e, t=T: e.transpose(
                      pb[6 + (t % 2)][0:128, 0:16], rankf[:, t * 128:(t + 1) * 128],
                      identf[0:16, 0:16]),
                      waits=[("dve", rf[0]), ("dve", rf[1]), ("act", rt_prev)])
                  rt_prev = ac(lambda e, t=T: e.activation(
                      rank_tm[:, t * 16:(t + 1) * 16], pb[6 + (t % 2)][0:128, 0:16], AF.Copy),
                      waits=[("pe", t_pe)])
              oh = tks[:, 0:16]
              r1 = lnc[:, 10:11]
              over = lnc[:, 11:12]
              t1c = lnc[:, 12:13]
              t2c = lnc[:, 13:14]
              for T in range(NT):
                  for k2 in range(2):
                      cidx = 2 * T + k2
                      k0 = dv(lambda e, ci=cidx: e.tensor_single_scalar(
                          oh, iota, i12[:, ci:ci + 1], ALU.is_equal),
                          waits=[("act", rt_prev), ("dve", c.n["dve"])])
                      k0 = dv(lambda e, t=T: e.tensor_mul(oh, oh, rank_tm[:, t * 16:(t + 1) * 16]),
                              waits=[("dve", k0)])
                      k0 = dv(lambda e: e.reduce_sum(r1, oh, axis=AX), waits=[("dve", k0)])
                      k0 = dv(lambda e: e.tensor_scalar(over, r1, float(CCAP), None, ALU.is_gt),
                              waits=[("dve", k0)])
                      k0 = dv(lambda e, ci=cidx: e.scalar_tensor_tensor(
                          t1c, i12[:, ci:ci + 1], float(CCAP), r1, ALU.mult, ALU.add),
                          waits=[("dve", k0)])
                      k0 = dv(lambda e: e.tensor_scalar(t1c, t1c, -1.0, None, ALU.add),
                              waits=[("dve", k0)])
                      k0 = dv(lambda e: e.tensor_scalar(t2c, over, -1.0, 1.0, ALU.mult, ALU.add),
                              waits=[("dve", k0)])
                      k0 = dv(lambda e: e.tensor_mul(t1c, t1c, t2c), waits=[("dve", k0)])
                      k0 = dv(lambda e: e.scalar_tensor_tensor(
                          t1c, over, float(PADROW), t1c, ALU.mult, ALU.add), waits=[("dve", k0)])
                      k0 = dv(lambda e, ci=cidx: e.tensor_copy(idxc[:, ci:ci + 1], t1c),
                              waits=[("dve", k0)])
              idx_done = c.n["dve"]
              xnw = c.n["dxw"]
              stg = attn_g[:, 0:2048]
              for T in range(NT):
                  dma("sync", "dxb", stg, xntm_d[T * 128:(T + 1) * 128, :],
                      waits=[("dxw", xnw), ("dsc", c.n["dsc"]), ("pe", c.n["pe"])])
                  ldb = c.n["dxb"]
                  for k2 in range(2):
                      c.emit("gpsimd", lambda e, ci=2 * T + k2: e.indirect_dma_start(
                          xe_d, IndirectOffsetOnAxis(idxc[:, ci:ci + 1], 0), stg, None),
                          waits=[("dxb", ldb), ("dve", idx_done), ("dz", dz_total)],
                          inc="dsc", inc_n=16)
              route["scat_total"] = c.n["dsc"]
              route["rf"] = rf
              route["rt_act"] = rt_prev

        slab_guard = [0, 0, 0]
        GSEM = ["dga", "dgb", "dgc"]
        wqs_extra = [[("dxw", 0)]]
        wqs_first = [[]]
        proj_psum_guard = [0, 0]

        def proj(jcs, src_fm, src_stride, goff, dst_of, src_ready_act):
            war = [("pe", c.n["pe"]), ("dve", c.n["dve"])]
            evacs = []
            for si, jc in enumerate(jcs):
                par = si % 3
                ppar = si % 2
                gsem = GSEM[par]
                xw = wqs_extra[0] if par == 2 else []
                if par == 2:
                    wqs_extra[0] = []
                xw = xw + wqs_first[0]
                wqs_first[0] = []
                dma("gpsimd" if par == 1 else "sync", gsem, wqs[par], wqkv16_d[jc],
                    waits=[("pe", slab_guard[par]), ("dk", wq16_done)] + xw)
                ldt = c.n[gsem]
                for dc in range(16):
                    w = [(gsem, ldt), ("act", src_ready_act)] if dc == 0 else []
                    if dc == 0:
                        w.append(("dve", proj_psum_guard[ppar]))
                    mm = pe(lambda e, d=dc, p=par, pp=ppar: e.matmul(
                        pb[pp][:, 0:256], wqs[p][:, d * 128:(d + 1) * 128],
                        src_fm[:, d * src_stride + goff:d * src_stride + goff + 256],
                        start=(d == 0), stop=(d == 15), skip_group_check=True),
                        waits=w, inc=(dc == 15))
                slab_guard[par] = mm
                buf, col = dst_of(jc)
                ev = dv(lambda e, b=buf, cl=col, p=ppar, j=jc: e.tensor_scalar(
                    b[:, cl:cl + 256], pb[p][:, 0:256], bqkv[:, j:j + 1], None, ALU.add),
                    waits=[("pe", mm)] + war)
                proj_psum_guard[ppar] = ev
                evacs.append(ev)
            return evacs[-1]

        scB = [0, 0]
        ioB = [0, 0]
        kaB = [0, 0]
        vtB = [0, 0]
        sbfG = [0, 0]
        kdG = [0, 0]
        vtG = [0, 0]
        qdG = [0, 0]
        smG = [0, 0]
        Supd = [0] * H
        Scast = [0] * H

        def scan_chunk(cc, main, ready_dve, attn_guard_pe):
            for h in range(H):
                par = h % 2
                kcol = h * 256 + cc * 128
                mcol = h * 128
                if main:
                    Scast[h] = ac(lambda e, hh=h, p=par: e.activation(
                        sbf[p], Ssb[:, hh * 128:(hh + 1) * 128], AF.Copy),
                        waits=[("pe", sbfG[par]), ("dve", Supd[h])])
                    mm_sc = pe(lambda e, p=par, kc=kcol: e.matmul(
                        pbf(p), kb[:, kc:kc + 128], qb[:, kc:kc + 128],
                        start=True, stop=True, skip_group_check=True),
                        waits=[("dve", ready_dve), ("dve", scB[par])])
                    smG_new = dv(lambda e, p=par, mc=mcol: e.tensor_tensor(
                        sm[p], pbf(p), masks[:, mc:mc + 128], ALU.mult),
                        waits=[("pe", mm_sc), ("pe", smG[par])])
                    scB[par] = smG_new
                    qd_new = dv(lambda e, p=par, kc=kcol, mc=mcol: e.tensor_tensor(
                        qd[p], qb[:, kc:kc + 128], dqb[:, mc:mc + 128], ALU.mult),
                        waits=[("pe", qdG[par]), ("dve", ready_dve)])
                mm_kt = pe(lambda e, p=par, kc=kcol: e.transpose(
                    pbbf(4 + p), kb[:, kc:kc + 128], identb),
                    waits=[("dve", ready_dve), ("dve", kaB[par])])
                mm_vt = pe(lambda e, p=par, kc=kcol: e.transpose(
                    pbbf(6 + p), vb[:, kc:kc + 128], identb),
                    waits=[("dve", ready_dve), ("act", vtB[par])])
                kd_new = dv(lambda e, p=par, hh=h: e.tensor_scalar(
                    kdT[p], pbbf(4 + p), deck[:, hh:hh + 1], None, ALU.mult),
                    waits=[("pe", mm_kt), ("pe", kdG[par])])
                vt_new = ac(lambda e, p=par: e.activation(vT[p], pbbf(6 + p), AF.Copy),
                            waits=[("pe", mm_vt), ("pe", vtG[par])])
                vtB[par] = vt_new
                if main:
                    mm_io = pe(lambda e, p=par: e.matmul(
                        pbf(2 + p), vT[p], sm[p], start=True, stop=False, skip_group_check=True),
                        waits=[("act", vt_new), ("dve", smG_new), ("act", ioB[par])])
                    mm_io = pe(lambda e, p=par: e.matmul(
                        pbf(2 + p), sbf[p], qd[p], start=False, stop=True, skip_group_check=True),
                        waits=[("act", Scast[h]), ("dve", qd_new)])
                    sbfG[par] = mm_io
                    qdG[par] = mm_io
                    smG[par] = mm_io
                    ioB[par] = ac(lambda e, p=par, kc=kcol: e.activation(
                        attn_g[:, kc:kc + 128], pbf(2 + p), AF.Copy),
                        waits=[("pe", mm_io), ("pe", attn_guard_pe)] + attn_g_zwait[0])
                    attn_g_zwait[0] = []
                mm_A = pe(lambda e, p=par: e.matmul(
                    pb[4 + p][:, 64:192], kdT[p], vT[p], start=True, stop=True, skip_group_check=True),
                    waits=[("dve", kd_new), ("act", vt_new)])
                kdG[par] = mm_A
                vtG[par] = mm_A
                Supd[h] = dv(lambda e, hh=h, p=par: e.scalar_tensor_tensor(
                    Ssb[:, hh * 128:(hh + 1) * 128], Ssb[:, hh * 128:(hh + 1) * 128],
                    float(LAMC[hh]), pb[4 + p][:, 64:192], ALU.mult, ALU.add),
                    waits=[("pe", mm_A)] + ([("act", Scast[h])] if main else [("dve", S_zero)]))
                kaB[par] = Supd[h]

        wqs_first[0] = [("dve", c.n["dve"]), ("act", c.n["act"]), ("pe", c.n["pe"])]
        if PHASES < 2:
            emit_dw_routing()
        xnp_guard = [0]
        for pg in range(NG if PHASES >= 2 else 0):
            for t2 in range(2):
                ln_tile(xp[(pg * 2 + t2) * 128:(pg * 2 + t2 + 1) * 128, :],
                        xb_guard, False, (lambda d, tl=t2: d * 256 + tl * 128), 0)
                xb_guard = [("dve", c.n["dve"]), ("act", c.n["act"]), ("pe", c.n["pe"])]
            ev = proj(range(16, 48), xnp, 256, 0,
                      lambda jc: (kb, (jc - 16) * 256) if jc < 32 else (vb, (jc - 32) * 256),
                      c.n["act"])
            if pg == 0:
                emit_dw_routing()
                kaB[0] = max(kaB[0], route["rf"][0])
                kaB[1] = max(kaB[1], route["rf"][1])
                vtB[0] = max(vtB[0], route["rt_act"])
                vtB[1] = max(vtB[1], route["rt_act"])
                attn_g_zwait[0] = attn_g_zwait[0] + [("dsc", route["scat_total"])]
            for cc in range(2):
                scan_chunk(cc, False, ev, 0)
            xb_guard = [("pe", c.n["pe"]), ("dve", c.n["dve"]), ("act", c.n["act"])]
        Sgate = dv(lambda e: e.tensor_single_scalar(Ssb, Ssb, gatev, ALU.mult),
                   waits=[("dve", max(Supd)), ("dc", consts)])
        for h in range(H):
            Supd[h] = Sgate

        outproj_done_pe = [0]
        for g in range(NG if PHASES >= 3 else 0):
            ev = proj(range(0, 48), xn_fm, TPC, g * 256,
                      lambda jc: (qb, jc * 256) if jc < 16 else
                                 ((kb, (jc - 16) * 256) if jc < 32 else (vb, (jc - 32) * 256)),
                      xn_main_done_act)
            scB[0] = max(scB[0], proj_psum_guard[0])
            scB[1] = max(scB[1], proj_psum_guard[1])
            for cc in range(2 if KP3 >= 2 else 0):
                scan_chunk(cc, True, ev, outproj_done_pe[0])
            attn_done_act = c.n["act"]
            scan_done_dve = c.n["dve"]
            op_stop = [0] * 8
            for kc in range(16 if KP3 >= 3 else 0):
                par = kc % 3
                gsem = GSEM[par]
                dma("gpsimd" if par == 1 else "sync", gsem, wqs[par], wout16_d[kc],
                    waits=[("pe", slab_guard[par]), ("dk", wo16_done)])
                ldt = c.n[gsem]
                for tch in range(2):
                    for dcol in range(4):
                        bi = tch * 4 + dcol
                        w = []
                        if kc == 0:
                            w = [("act", attn_done_act), ("dve", scan_done_dve),
                                 ("dve", opB_guard[bi])]
                        if dcol == 0 and tch == 0:
                            w.append((gsem, ldt))
                        last_of_kc = (tch == 1 and dcol == 3)
                        mm = pe(lambda e, k2=kc, t2=tch, d2=dcol, p=par, b=bi: e.matmul(
                            pb[b][:, 0:512],
                            attn_g[:, k2 * 256 + t2 * 128:k2 * 256 + t2 * 128 + 128],
                            wqs[p][:, d2 * 512:(d2 + 1) * 512],
                            start=(k2 == 0), stop=(k2 == 15), skip_group_check=True),
                            waits=w, inc=(kc == 15 or last_of_kc))
                        if kc == 15 and mm is not None:
                            op_stop[bi] = mm
                slab_guard[par] = c.n["pe"]
            for tch in range(2 if KP3 >= 3 else 0):
                for dcol in range(4):
                    bi = tch * 4 + dcol
                    col = (g * 2 + tch) * D + dcol * 512
                    opB_guard[bi] = dv(lambda e, cl=col, b=bi: e.tensor_tensor(
                        out_acc[:, cl:cl + 512], out_acc[:, cl:cl + 512], pb[b][:, 0:512], ALU.add),
                        waits=[("pe", op_stop[bi]), ("dve", out_init[g * 2 + tch])])
            outproj_done_pe[0] = c.n["pe"]
            for i2 in range(2):
                proj_psum_guard[i2] = max(proj_psum_guard[i2], opB_guard[i2])
                scB[i2] = max(scB[i2], opB_guard[i2])
            ioB[0] = ioB[1] = c.n["act"]
            kaB[0] = max(kaB[0], opB_guard[4])
            kaB[1] = max(kaB[1], opB_guard[5])
            vtB[0] = vtB[1] = c.n["act"]

        PEa, DVa, ACa = c.n["pe"], c.n["dve"], c.n["act"]
        barrier = [("pe", PEa), ("dve", DVa), ("act", ACa)]

        ab_guard = [ACa, ACa, DVa, DVa]
        y_guard = [DVa, DVa]
        sl_guard = [0, 0]
        w13_guard = [0, 0]
        w2_guard = [0, 0]
        hbuf_guard = [PEa]

        def mlp(nchunks, w13src, w2src, jbase, use_dw, first):
            for j in range(nchunks):
                par = j % 2
                wsem = "dwa" if par == 0 else "dwb"
                wsem2 = "dwa2" if par == 0 else "dwb2"
                wts = [("pe", w13_guard[par])] + (barrier if first and j < 2 else [])
                dma("sync", wsem, w13[2 * par + 0], w13src[jbase + j][:, 0:2048], waits=wts)
                dma("gpsimd", wsem2, w13[2 * par + 1], w13src[jbase + j][:, 2048:4096], waits=wts)
                ldt = c.n[wsem]
                ldt2 = c.n[wsem2]
                for th in range(2):
                    pab = (j * 2 + th) % 2
                    stops = []
                    for which in range(2):
                        bank = pab + 2 * which
                        for dc in range(16):
                            w = []
                            if dc == 0:
                                w = [(wsem, ldt) if which == 0 else (wsem2, ldt2),
                                     (("act", ab_guard[bank]) if which == 0
                                      else ("dve", ab_guard[bank]))]
                            mm = pe(lambda e, d=dc, b=bank, p=par, wh=which, t=th: e.matmul(
                                pb[b][:, 0:512], w13[2 * p + wh][:, d * 128:(d + 1) * 128],
                                xn_fm[:, d * TPC + t * 512:d * TPC + t * 512 + 512],
                                start=(d == 0), stop=(d == 15), skip_group_check=True),
                                waits=w, inc=(dc == 15))
                            if dc == 15:
                                stops.append(mm)
                    w13_guard[par] = stops[-1]
                    sp = th
                    a_sl = ac(lambda e, s=sp, b=pab: e.activation(silu_sb[s], pb[b][:, 0:512], AF.Silu),
                              waits=[("pe", stops[0]), ("dve", sl_guard[sp])])
                    ab_guard[pab] = a_sl
                    hcol = j * TPC + th * 512
                    h1 = dv(lambda e, s=sp, b=2 + pab, cl=hcol: e.tensor_tensor(
                        hbuf[:, cl:cl + 512], silu_sb[s], pb[b][:, 0:512], ALU.mult),
                        waits=[("act", a_sl), ("pe", stops[1]), ("pe", hbuf_guard[0])])
                    ab_guard[2 + pab] = h1
                    sl_guard[sp] = h1
            h_done = c.n["dve"]
            for dcol in range(4):
                par = dcol % 2
                w2sem = "dw2a" if par == 0 else "dw2b"
                dma("gpsimd", w2sem, w2g[par], w2src[dcol],
                    waits=[("pe", w2_guard[par])] + (barrier if first and dcol < 2 else []))
                ldt = c.n[w2sem]
                for tch in range(NT):
                    yb = 4 + (tch % 2)
                    for j in range(nchunks):
                        w = []
                        if j == 0:
                            w = [(w2sem, ldt), ("dve", y_guard[tch % 2]), ("dve", h_done)]
                        mm = pe(lambda e, j2=j, b=yb, t=tch, d=dcol, p=par: e.matmul(
                            pb[b][:, 0:512], hbuf[:, j2 * TPC + t * 128:j2 * TPC + t * 128 + 128],
                            w2g[p][:, j2 * 512:(j2 + 1) * 512],
                            start=(j2 == 0), stop=(j2 == nchunks - 1), skip_group_check=True),
                            waits=w, inc=(j == nchunks - 1))
                    col = tch * D + dcol * 512
                    if use_dw is None:
                        y_guard[tch % 2] = dv(lambda e, cl=col, b=yb: e.tensor_tensor(
                            out_acc[:, cl:cl + 512], out_acc[:, cl:cl + 512], pb[b][:, 0:512], ALU.add),
                            waits=[("pe", mm)])
                    else:
                        dcol_w = use_dw(tch)
                        y_guard[tch % 2] = dv(lambda e, cl=col, b=yb, wc=dcol_w: e.scalar_tensor_tensor(
                            out_acc[:, cl:cl + 512], pb[b][:, 0:512], wc,
                            out_acc[:, cl:cl + 512], ALU.mult, ALU.add),
                            waits=[("pe", mm), ("dve", route["dw_done"])])
                w2_guard[par] = c.n["pe"]
            hbuf_guard[0] = c.n["pe"]

        if PHASES >= 4:
            mlp(11, sw13_h, sw2_h[0], 0, None, True)
            mlp(11, sw13_h, sw2_h[1], 11, None, False)

            PEs, DVs, ACs = c.n["pe"], c.n["dve"], c.n["act"]
            barrier2 = [("pe", PEs), ("dve", DVs), ("act", ACs)]
            xet_guard = [0]
            xef_guard = [0]
            trB = [ACs, ACs]
            sp_ab = [ACs, ACs, DVs, DVs]
            sp_sl = [0, 0]
            sp_y = [ACs, ACs]
            ye_guard = [0]
            def load_transpose(ex, exw):
                for sub in range(2):
                    dma("sync", "dxe", xe_tm[:, sub * 2048:(sub + 1) * 2048],
                        xe_d[ex * CCAP + sub * 128:ex * CCAP + (sub + 1) * 128, :],
                        waits=[("dsc", route["scat_total"]), ("pe", xet_guard[0])] + exw)
                ldx = c.n["dxe"]
                aev = 0
                for sub in range(2):
                    for dcj in range(16):
                        par = (sub * 16 + dcj) % 2
                        t_pe = pe(lambda e, s2=sub, d2=dcj: e.transpose(
                            pbbf(6 + ((s2 * 16 + d2) % 2)),
                            xe_tm[:, s2 * 2048 + d2 * 128:s2 * 2048 + (d2 + 1) * 128], identb),
                            waits=[("dxe", ldx), ("act", trB[par]), ("pe", xef_guard[0])])
                        aev = ac(lambda e, s2=sub, d2=dcj, p2=par: e.activation(
                            xe_fm[:, d2 * 256 + s2 * 128:d2 * 256 + s2 * 128 + 128],
                            pbbf(6 + p2), AF.Copy), waits=[("pe", t_pe)])
                        trB[par] = aev
                xet_guard[0] = c.n["pe"]
                return aev

            nxt_done = load_transpose(0, barrier2)
            for ex in range(NE):
                xef_done = nxt_done
                for j in range(NJ):
                    par = j % 2
                    wsem = "dwa" if par == 0 else "dwb"
                    wsem2 = "dwa2" if par == 0 else "dwb2"
                    wts = [("pe", w13_guard[par])]
                    dma("sync", wsem, w13[2 * par + 0], w13_h[ex][j][:, 0:2048], waits=wts)
                    dma("gpsimd", wsem2, w13[2 * par + 1], w13_h[ex][j][:, 2048:4096], waits=wts)
                    ldt = c.n[wsem]
                    ldt2 = c.n[wsem2]
                    pab = j % 2
                    stops = []
                    for which in range(2):
                        bank = pab + 2 * which
                        for dc in range(16):
                            w = []
                            if dc == 0:
                                w = [(wsem, ldt) if which == 0 else (wsem2, ldt2),
                                     ("act", xef_done),
                                     (("act", sp_ab[bank]) if which == 0
                                      else ("dve", sp_ab[bank]))]
                            mm = pe(lambda e, d=dc, b=bank, p=par, wh=which: e.matmul(
                                pb[b][:, 0:256], w13[2 * p + wh][:, d * 128:(d + 1) * 128],
                                xe_fm[:, d * 256:(d + 1) * 256],
                                start=(d == 0), stop=(d == 15), skip_group_check=True),
                                waits=w, inc=(dc == 15))
                            if dc == 15:
                                stops.append(mm)
                    w13_guard[par] = stops[-1]
                    a_sl = ac(lambda e, s2=pab, b=pab: e.activation(
                        silu_sb[s2][:, 0:256], pb[b][:, 0:256], AF.Silu),
                        waits=[("pe", stops[0]), ("dve", sp_sl[pab])])
                    sp_ab[pab] = a_sl
                    h1 = dv(lambda e, s2=pab, b=2 + pab, cl=j * 256: e.tensor_tensor(
                        he[:, cl:cl + 256], silu_sb[s2][:, 0:256], pb[b][:, 0:256], ALU.mult),
                        waits=[("act", a_sl), ("pe", stops[1])])
                    sp_ab[2 + pab] = h1
                    sp_sl[pab] = h1
                h_done = c.n["dve"]
                xef_guard[0] = c.n["pe"]
                if ex + 1 < NE:
                    nxt_done = load_transpose(ex + 1, [])
                for dcol in range(4):
                    par = dcol % 2
                    w2sem = "dw2a" if par == 0 else "dw2b"
                    dma("gpsimd", w2sem, w2g[par], w2_h[ex][dcol], waits=[("pe", w2_guard[par])])
                    ldt = c.n[w2sem]
                    for csub in range(2):
                        yb = 4 + ((dcol * 2 + csub) % 2)
                        for j in range(NJ):
                            w = []
                            if j == 0:
                                w = [(w2sem, ldt), ("act", sp_y[(dcol * 2 + csub) % 2]),
                                     ("dve", h_done)]
                            mm = pe(lambda e, j2=j, b=yb, cs=csub, d=dcol, p=par: e.matmul(
                                pb[b][:, 0:512],
                                he[:, j2 * 256 + cs * 128:j2 * 256 + cs * 128 + 128],
                                w2g[p][:, j2 * 512:(j2 + 1) * 512],
                                start=(j2 == 0), stop=(j2 == NJ - 1), skip_group_check=True),
                                waits=w, inc=(j == NJ - 1))
                        sp_y[(dcol * 2 + csub) % 2] = ac(
                            lambda e, cs=csub, d=dcol, b=yb: e.activation(
                                ye_sb[:, cs * 2048 + d * 512:cs * 2048 + d * 512 + 512],
                                pb[b][:, 0:512], AF.Copy),
                            waits=[("pe", mm), ("dye", ye_guard[0]), ("dz", dz_total)])
                    w2_guard[par] = c.n["pe"]
                ye_done_act = c.n["act"]
                for csub in range(2):
                    dma("gpsimd", "dye", ye_d[ex * CCAP + csub * 128:ex * CCAP + (csub + 1) * 128, :],
                        ye_sb[:, csub * 2048:(csub + 1) * 2048], waits=[("act", ye_done_act)])
                ye_guard[0] = c.n["dye"]

            ye_total = c.n["dye"]
            PEc, DVc, ACc = c.n["pe"], c.n["dve"], c.n["act"]
            barrier3 = [("pe", PEc), ("dve", DVc), ("act", ACc)]
            comb_guard = [0]
            for T in range(NT):
                gt = []
                for k2 in range(2):
                    g_ = c.emit("gpsimd", lambda e, ci=2 * T + k2, kk=k2: e.indirect_dma_start(
                        ybuf[kk], None, ye_d, IndirectOffsetOnAxis(idxc[:, ci:ci + 1], 0)),
                        waits=[("dye", ye_total), ("dve", comb_guard[0])]
                              + (barrier3 if T == 0 else []),
                        inc="dgt", inc_n=16)
                    gt.append(g_)
                for k2 in range(2):
                    comb_guard[0] = dv(lambda e, t=T, ci=2 * T + k2, kk=k2: e.scalar_tensor_tensor(
                        out_acc[:, t * D:(t + 1) * D], ybuf[kk], wcols[:, ci:ci + 1],
                        out_acc[:, t * D:(t + 1) * D], ALU.mult, ALU.add),
                        waits=[("dgt", gt[1]), ("dve", c.n["dve"])])

        total_dve = c.n["dve"]
        for tt in range(NT):
            dst = y_h[tt * 128:(tt + 1) * 128, :]
            c.emit("sync", lambda e, d=dst, t=tt: e.dma_start(d, out_acc[:, t * D:(t + 1) * D]),
                   waits=[("dve", total_dve)], inc="do", inc_n=16)
        c.emit("sync", lambda e: e.wait_ge(sems["do"], 16 * NT), waits=[])

    opB_guard = [0] * 8

    from contextlib import ExitStack
    with ExitStack() as st:
        for s_name in SEMS:
            sems[s_name] = st.enter_context(nc.semaphore(s_name))
        with nc.Block() as blk:
            final = {}

            def run(kind, eng):
                cc = S(kind, eng, sems)
                for i in range(8):
                    opB_guard[i] = 0
                sched(cc)
                final[kind] = dict(cc.n)

            blk.sync(lambda e: run("sync", e))
            blk.gpsimd(lambda e: run("gpsimd", e))
            blk.tensor(lambda e: run("tensor", e))
            blk.vector(lambda e: run("vector", e))
            blk.scalar(lambda e: run("scalar", e))
            kinds = list(final)
            for k in kinds[1:]:
                assert final[k] == final[kinds[0]], "schedule replay diverged"
    return nc


_NC_CACHE = None


def kernel(L0_attn_x, L0_attn_w_qkv, L0_attn_b_qkv, L0_attn_w_out, L0_attn_b_out,
           L0_ffn_gate_w, L0_ffn_w1, L0_ffn_w2, L0_ffn_w3,
           L0_ffn_sw1, L0_ffn_sw2, L0_ffn_sw3):
    global _NC_CACHE
    X = np.ascontiguousarray(np.asarray(L0_attn_x, np.float32))
    imgs = _images(L0_attn_w_qkv, L0_attn_b_qkv, L0_attn_w_out, L0_attn_b_out,
                   L0_ffn_gate_w, L0_ffn_w1, L0_ffn_w2, L0_ffn_w3,
                   L0_ffn_sw1, L0_ffn_sw2, L0_ffn_sw3)
    consts = _host_consts()
    shared = {**imgs, **consts}
    in_maps = []
    for cidx in range(8):
        b, half = cidx // 2, cidx % 2
        xo = np.ascontiguousarray(X[b, half * 1024:(half + 1) * 1024, :])
        xpre = np.ascontiguousarray(X[b, 0:1024, :]) if half else xo
        gv = np.full((128, 1), float(half), np.float32)
        in_maps.append({"xo": xo, "xp": xpre, "gatev": gv, **shared})
    if _NC_CACHE is None:
        _NC_CACHE = build_nc()
    res = run_bass_kernel_spmd(_NC_CACHE, in_maps, core_ids=list(range(8)))
    out = np.empty((4, 2048, 2048), np.float32)
    for cidx in range(8):
        b, half = cidx // 2, cidx % 2
        out[b, half * 1024:(half + 1) * 1024, :] = res.results[cidx]["y"]
    return out


# version 12
# speedup vs baseline: 1.0355x; 1.0355x over previous
import os
import numpy as np
import ml_dtypes
import concourse.bass as bass
import concourse.mybir as mybir
from concourse.bass import IndirectOffsetOnAxis
from concourse.bass_utils import run_bass_kernel_spmd

F32 = mybir.dt.float32
BF16 = mybir.dt.bfloat16
U8 = mybir.dt.uint8
AF = mybir.ActivationFunctionType
ALU = mybir.AluOpType
AX = mybir.AxisListType.X
BF = ml_dtypes.bfloat16

D = 2048
H = 16
HD = 128
CH = 128
TPC = 1024
NT = TPC // 128
NG = TPC // 256
NE = 16
EI = 1408
NJ = EI // 128
SI = 2816
NSJ = SI // 128
EPS = 1e-5
CCAP = 256
PADROW = NE * CCAP

_slopes = (2.0 ** (-0.5)) ** (np.arange(1, H + 1, dtype=np.float64))
LAM = np.exp(-_slopes.astype(np.float32)).astype(np.float32)
LAMC = (LAM.astype(np.float64) ** CH).astype(np.float32)


def _host_consts():
    t = np.arange(CH, dtype=np.float64)
    lam = LAM.astype(np.float64)
    masks = np.zeros((128, H * 128), np.float64)
    dqb = np.zeros((128, H * 128), np.float64)
    deck = np.zeros((128, H), np.float64)
    for h in range(H):
        diff = t[None, :] - t[:, None]
        m = np.where(diff >= 0, lam[h] ** np.maximum(diff, 0), 0.0)
        masks[:, h * 128:(h + 1) * 128] = m
        dqb[:, h * 128:(h + 1) * 128] = np.broadcast_to(lam[h] ** (t + 1.0), (128, 128))
        deck[:, h] = lam[h] ** (127.0 - t)
    iota = np.broadcast_to(np.arange(16, dtype=np.float32), (128, 16)).copy()
    ub = np.zeros((8, 128, 1024), np.float64)
    for T in range(8):
        tp = T * 128 + np.arange(128)
        ub[T] = (tp[:, None] <= np.arange(1024)[None, :]).astype(np.float64)
    return {
        "ubig": ub.astype(BF),
        "masks": masks.astype(BF),
        "dqb": dqb.astype(BF),
        "deck": deck.astype(np.float32),
        "identf": np.eye(128, dtype=np.float32),
        "identb": np.eye(128, dtype=BF),
        "ones1": np.ones((1, 128), np.float32),
        "zeros": np.zeros((128, 2048), BF),
        "iota": iota,
        "iotaB": (iota + 1e6).astype(np.float32),
    }


def _images(L0_attn_w_qkv, L0_attn_b_qkv, L0_attn_w_out, L0_attn_b_out,
            L0_ffn_gate_w, L0_ffn_w1, L0_ffn_w2, L0_ffn_w3,
            L0_ffn_sw1, L0_ffn_sw2, L0_ffn_sw3):
    asc = np.ascontiguousarray
    wqkv = asc(np.asarray(L0_attn_w_qkv, np.float32).reshape(48, 128, 16, 128)
               .transpose(0, 3, 2, 1).reshape(48, 128, 2048))
    bqkv = asc(np.asarray(L0_attn_b_qkv, np.float32).reshape(48, 128).T)
    wout = asc(np.asarray(L0_attn_w_out, np.float32).T.reshape(16, 128, 2048))
    boutr = asc(np.asarray(L0_attn_b_out, np.float32).reshape(1, 2048))
    gateb = asc(np.asarray(L0_ffn_gate_w, np.float32).reshape(16, 16, 128)
                .transpose(2, 1, 0).reshape(128, 256))
    w1 = np.asarray(L0_ffn_w1)
    w3 = np.asarray(L0_ffn_w3)
    w2 = np.asarray(L0_ffn_w2)
    sw1 = np.asarray(L0_ffn_sw1)
    sw3 = np.asarray(L0_ffn_sw3)
    sw2 = np.asarray(L0_ffn_sw2)
    w1i_ = w1.reshape(16, 11, 128, 16, 128).transpose(0, 1, 4, 3, 2).reshape(16, 11, 128, 2048)
    w3i_ = w3.reshape(16, 11, 128, 16, 128).transpose(0, 1, 4, 3, 2).reshape(16, 11, 128, 2048)
    w13i = asc(np.concatenate([w1i_, w3i_], axis=-1))
    w2i = asc(w2.reshape(16, 4, 512, 11, 128).transpose(0, 1, 4, 3, 2)
              .reshape(16, 4, 128, 5632))
    sw1i_ = sw1.reshape(22, 128, 16, 128).transpose(0, 3, 2, 1).reshape(22, 128, 2048)
    sw3i_ = sw3.reshape(22, 128, 16, 128).transpose(0, 3, 2, 1).reshape(22, 128, 2048)
    sw13i = asc(np.concatenate([sw1i_, sw3i_], axis=-1))
    sw2i = asc(sw2.reshape(4, 512, 2, 11, 128).transpose(2, 0, 4, 3, 1)
               .reshape(2, 4, 128, 5632))
    return dict(wqkv=wqkv, bqkv=bqkv, wout=wout, boutr=boutr, gateb=gateb,
                w13i=w13i, w2i=w2i, sw13i=sw13i, sw2i=sw2i)


SEMS = ["dc", "dxx", "dwa", "dwb", "dw2a", "dw2b", "dga", "dgb", "do", "pe", "dve", "act",
        "dua", "dub", "dxw", "dxb", "dsc", "dxe", "dye", "dgt", "dz", "dk", "dgc", "dwa2", "dwb2", "dxx2"]


class S:

    def __init__(self, kind, eng, sems):
        self.kind = kind
        self.e = eng
        self.sems = sems
        self.n = {k: 0 for k in SEMS}
        self.waited = {k: 0 for k in SEMS}

    def emit(self, ek, fn=None, waits=(), inc=None, inc_n=1):
        for s, v in waits:
            assert v <= self.n[s], f"wait-on-future {s}: {v} > {self.n[s]}"
        if self.kind == ek:
            for s, v in waits:
                if v > self.waited[s]:
                    self.e.wait_ge(self.sems[s], v)
                    self.waited[s] = v
            ins = fn(self.e) if fn is not None else None
            if inc is not None and ins is not None:
                ins.then_inc(self.sems[inc], inc_n)
        if inc is not None:
            self.n[inc] += inc_n
        return self.n[inc] if inc is not None else None


PHASES = int(os.environ.get("K_PHASES", "4"))
KSUB = int(os.environ.get("K_SUB", "9"))
KLN = int(os.environ.get("K_LN", "9"))
KP3 = int(os.environ.get("K_P3", "9"))


def build_nc():
    nc = bass.Bass("TRN2", target_bir_lowering=False)
    dt = nc.dram_tensor
    xo = dt("xo", [TPC, D], F32, kind="ExternalInput").ap()
    xp = dt("xp", [TPC, D], F32, kind="ExternalInput").ap()
    gatev_h = dt("gatev", [128, 1], F32, kind="ExternalInput").ap()
    wqkv_h = dt("wqkv", [48, 128, 2048], F32, kind="ExternalInput").ap()
    bqkv_h = dt("bqkv", [128, 48], F32, kind="ExternalInput").ap()
    wout_h = dt("wout", [16, 128, 2048], F32, kind="ExternalInput").ap()
    boutr_h = dt("boutr", [1, 2048], F32, kind="ExternalInput").ap()
    gateb_h = dt("gateb", [128, 256], F32, kind="ExternalInput").ap()
    w13_h = dt("w13i", [16, 11, 128, 4096], BF16, kind="ExternalInput").ap()
    w2_h = dt("w2i", [16, 4, 128, 5632], BF16, kind="ExternalInput").ap()
    sw13_h = dt("sw13i", [22, 128, 4096], BF16, kind="ExternalInput").ap()
    sw2_h = dt("sw2i", [2, 4, 128, 5632], BF16, kind="ExternalInput").ap()
    masks_h = dt("masks", [128, 2048], BF16, kind="ExternalInput").ap()
    dqb_h = dt("dqb", [128, 2048], BF16, kind="ExternalInput").ap()
    deck_h = dt("deck", [128, 16], F32, kind="ExternalInput").ap()
    identf_h = dt("identf", [128, 128], F32, kind="ExternalInput").ap()
    identb_h = dt("identb", [128, 128], BF16, kind="ExternalInput").ap()
    ones1_h = dt("ones1", [1, 128], F32, kind="ExternalInput").ap()
    iota_h = dt("iota", [128, 16], F32, kind="ExternalInput").ap()
    iotaB_h = dt("iotaB", [128, 16], F32, kind="ExternalInput").ap()
    ubig_h = dt("ubig", [8, 128, 1024], BF16, kind="ExternalInput").ap()
    zeros_h = dt("zeros", [128, 2048], BF16, kind="ExternalInput").ap()
    y_h = dt("y", [TPC, D], F32, kind="ExternalOutput").ap()
    xe_d = dt("xe_d", [PADROW + 128, D], BF16, kind="Internal").ap()
    wqkv16_d = dt("wqkv16_d", [48, 128, 2048], BF16, kind="Internal").ap()
    wout16_d = dt("wout16_d", [16, 128, 2048], BF16, kind="Internal").ap()
    ye_d = dt("ye_d", [PADROW + 128, D], BF16, kind="Internal").ap()
    xntm_d = dt("xntm_d", [TPC, D], BF16, kind="Internal").ap()

    alloc = nc.alloc_sbuf_tensor
    xn_fm = alloc("xn_fm", [128, 16 * TPC], BF16).ap()
    out_acc = alloc("out_acc", [128, NT * D], F32).ap()
    bout_bc = alloc("bout_bc", [128, D], BF16).ap()
    masks = alloc("masksb", [128, 2048], BF16).ap()
    dqb = alloc("dqbb", [128, 2048], BF16).ap()
    Ssb = alloc("Ssb", [128, H * 128], F32).ap()
    deck = alloc("deckb", [128, 16], F32).ap()
    identf = alloc("identfb", [128, 128], F32).ap()
    identb = alloc("identbb", [128, 128], BF16).ap()
    ones1 = alloc("ones1b", [1, 128], F32).ap()
    bqkv = alloc("bqkvb", [128, 48], F32).ap()
    gateb = alloc("gatebb", [128, 256], F32).ap()
    iota = alloc("iotab", [128, 16], F32).ap()
    iotaB = alloc("iotaBb", [128, 16], F32).ap()
    boutr = alloc("boutrb", [1, 2048], F32).ap()
    gatev = alloc("gatevb", [128, 1], F32).ap()
    gs = alloc("gsb", [128, NT * 16], F32).ap()
    dw = alloc("dwb_tm", [128, NT * 16], F32).ap()
    lnc = alloc("lncol", [128, 16], F32).ap()
    tks = alloc("tkscratch", [128, 64], F32).ap()
    rank_tm = alloc("rank_tm", [128, 128], F32).ap()
    i12 = alloc("i12b", [128, 16], F32).ap()
    wcols = alloc("wcolsb", [128, 16], F32).ap()
    idxc = alloc("idxcb", [128, 16], mybir.dt.int32).ap()

    AR_BYTES = 74240
    arena = alloc("arena", [128, AR_BYTES], U8)

    def av(off, cols, dtp, rows=128):
        bw = 4 if dtp == F32 else (2 if dtp == BF16 else 1)
        ap_ = arena.ap()[0:rows, off:off + cols * bw].bitcast(dtp)
        return ap_, off + cols * bw

    o = 0
    xb_t, o = av(o, 2048, F32)
    xnt, o = av(o, 2048, F32)
    scr2, o = av(o, 256, F32)
    qb_off = o
    qb, o = av(o, 4096, BF16)
    kb, o = av(o, 4096, BF16)
    vb, o = av(o, 4096, BF16)
    xnp_off = o
    xnp, o = av(o, 4096, BF16)
    wqs_off = o
    wqs0, o = av(o, 2048, BF16)
    wqs1, o = av(o, 2048, BF16)
    attn_g_off = o
    attn_g, o = av(o, 4096, BF16)
    kdT0, o = av(o, 128, BF16)
    kdT1, o = av(o, 128, BF16)
    vT0, o = av(o, 128, BF16)
    vT1, o = av(o, 128, BF16)
    qd0, o = av(o, 128, BF16)
    qd1, o = av(o, 128, BF16)
    sm0_off = o
    sm0, o = av(o, 128, BF16)
    sm1, o = av(o, 128, BF16)
    sbf0, o = av(o, 128, BF16)
    sbf1, o = av(o, 128, BF16)
    xnb_off = o
    xnb, o = av(o, 2048, BF16)
    assert o <= AR_BYTES, o
    oq = attn_g_off
    uslab = []
    for k in range(2):
        w, oq = av(oq, 1024, BF16)
        uslab.append(w)
    rankf, _ = av(oq, 1024, F32, rows=16)
    maskb, _ = av(sm0_off, 128, BF16)
    wqs2, _ = av(xnb_off, 2048, BF16)
    xnt2, _ = av(attn_g_off, 2048, F32)
    xb2, _ = av(xnp_off, 2048, F32)
    xbP = [xb_t, xb2]
    xntP = [xnt, xnt2]
    o = 0
    hbuf, o = av(o, NJ * TPC, BF16)
    w13 = []
    w13P = []
    for k in range(4):
        if k % 2 == 0:
            w13P.append(arena.ap()[0:128, o:o + 8192].bitcast(BF16))
        w, o = av(o, 2048, BF16)
        w13.append(w)
    w2g0, o = av(o, 5632, BF16)
    w2g1, o = av(o, 5632, BF16)
    silu0, o = av(o, 512, BF16)
    silu1, o = av(o, 512, BF16)
    assert o <= AR_BYTES, o
    os_ = 0
    xe_tm, os_ = av(os_, 2 * 2048, BF16)
    xe_fm, os_ = av(os_, 4096, BF16)
    he, os_ = av(os_, NJ * 256, BF16)
    assert os_ <= 22528, os_
    ye_sb, o = av(o, 2 * 2048, BF16)
    assert o <= AR_BYTES, o
    ybuf = [xe_tm[:, 0:2048], xe_tm[:, 2048:4096]]
    zb16 = xb_t.bitcast(BF16)

    kdT = [kdT0, kdT1]
    vT = [vT0, vT1]
    qd = [qd0, qd1]
    sm = [sm0, sm1]
    sbf = [sbf0, sbf1]
    wqs = [wqs0, wqs1, wqs2]
    w2g = [w2g0, w2g1]
    silu_sb = [silu0, silu1]

    pb = [nc.alloc_psum_tensor(f"pb{i}", [128, 512], F32).ap() for i in range(8)]

    def pbf(i, cols=128):
        return pb[i][:, 0:cols]

    def pbbf(i):
        return pb[i][:, 0:64].bitcast(BF16)

    sems = {}

    def sched(c: S):
        def dma(queue, sem, dst, src, waits=()):
            return c.emit(queue, lambda e: e.dma_start(dst, src), waits, sem, 16)
        pe = lambda fn, waits=(), inc=True: c.emit("tensor", fn, waits, "pe" if inc else None)
        dv = lambda fn, waits=(), inc=True: c.emit("vector", fn, waits, "dve" if inc else None)
        ac = lambda fn, waits=(), inc=True: c.emit("scalar", fn, waits, "act" if inc else None)

        for dst, src in [(masks, masks_h), (dqb, dqb_h), (deck, deck_h),
                         (identf, identf_h), (identb, identb_h), (ones1, ones1_h),
                         (bqkv, bqkv_h), (gateb, gateb_h), (iota, iota_h),
                         (iotaB, iotaB_h), (boutr, boutr_h), (gatev, gatev_h)]:
            dma("sync", "dc", dst, src)
        consts = c.n["dc"]
        for i in range(33):
            dma("gpsimd", "dz", xe_d[i * 128:(i + 1) * 128, :], zeros_h,
                waits=[("dc", consts)])
        dma("gpsimd", "dz", ye_d[PADROW:PADROW + 128, :], zeros_h,
            waits=[("dc", consts)])
        dz_total = c.n["dz"]
        attn_g_zwait = [[("dz", dz_total)]]
        for jc in list(range(16, 48)) + list(range(0, 16)):
            dma("gpsimd", "dk", wqkv16_d[jc], wqkv_h[jc])
        wq16_done = c.n["dk"]
        for kc in range(16):
            dma("gpsimd", "dk", wout16_d[kc], wout_h[kc])
        wo16_done = c.n["dk"]
        for dcol in range(4):
            pe(lambda e, d=dcol: e.matmul(pb[3][:, 0:512], ones1, boutr[:, d * 512:(d + 1) * 512],
                                          start=True, stop=True, skip_group_check=True),
               waits=[("dc", consts), ("act", c.n["act"])])
            ac(lambda e, d=dcol: e.activation(bout_bc[:, d * 512:(d + 1) * 512], pb[3][:, 0:512], AF.Copy),
               waits=[("pe", c.n["pe"])])
        bout_ready = c.n["act"]
        S_zero = dv(lambda e: e.memset(Ssb, 0.0))

        def ln_tile(x_src, tile_guard_waits, do_gate, dst_fm_col_of, gate_tt,
                    mid_cb=None, xb_t=xb_t, xnt=xnt, xsem="dxx"):
            dma("sync", xsem, xb_t, x_src, waits=tile_guard_waits)
            ld = c.n[xsem]
            r0 = dv(lambda e: e.reduce_sum(lnc[:, 0:1], xb_t, axis=AX), waits=[(xsem, ld)])
            a0 = ac(lambda e: e.activation(xnt, xb_t, AF.Square),
                    waits=[(xsem, ld), ("pe", c.n["pe"])])
            if mid_cb is not None:
                mid_cb()
            r3 = dv(lambda e: e.reduce_sum(lnc[:, 2:3], xnt, axis=AX),
                    waits=[("act", a0)])
            r1 = dv(lambda e: e.tensor_scalar(lnc[:, 1:2], lnc[:, 0:1], 1.0 / D, None, ALU.mult),
                    waits=[("dve", r3)])
            r5 = dv(lambda e: e.tensor_mul(lnc[:, 5:6], lnc[:, 1:2], lnc[:, 1:2]), waits=[("dve", r1)])
            r5 = dv(lambda e: e.tensor_scalar(lnc[:, 7:8], lnc[:, 2:3], 1.0 / D, EPS, ALU.mult, ALU.add),
                    waits=[("dve", r5)])
            r5 = dv(lambda e: e.tensor_sub(lnc[:, 6:7], lnc[:, 7:8], lnc[:, 5:6]), waits=[("dve", r5)])
            a1 = ac(lambda e: e.activation(lnc[:, 3:4], lnc[:, 6:7], AF.Sqrt),
                    waits=[("dve", r5)])
            r4 = dv(lambda e: e.reciprocal(lnc[:, 4:5], lnc[:, 3:4]), waits=[("act", a1)])
            r6 = dv(lambda e: e.tensor_mul(lnc[:, 8:9], lnc[:, 4:5], lnc[:, 4:5]), waits=[("dve", r4)])
            r7 = dv(lambda e: e.tensor_mul(lnc[:, 9:10], lnc[:, 6:7], lnc[:, 8:9]), waits=[("dve", r6)])
            r8 = dv(lambda e: e.tensor_scalar(lnc[:, 10:11], lnc[:, 9:10], -0.5, 1.5, ALU.mult, ALU.add),
                    waits=[("dve", r7)])
            r9 = dv(lambda e: e.tensor_mul(lnc[:, 11:12], lnc[:, 4:5], lnc[:, 10:11]), waits=[("dve", r8)])
            rA = dv(lambda e: e.tensor_scalar(xnt, xb_t, lnc[:, 1:2], lnc[:, 11:12],
                                              ALU.subtract, ALU.mult),
                    waits=[("dve", r9)])
            if do_gate:
                axn = ac(lambda e: e.activation(xnb, xnt, AF.Copy),
                         waits=[("dve", rA), ("dxw", c.n["dxw"])])
                dma("sync", "dxw", xntm_d[gate_tt * 128:(gate_tt + 1) * 128, :], xnb,
                    waits=[("act", axn)])
            for dc in range(16 if KLN >= 2 else 0):
                par = dc % 2
                bank = par
                tw = [("dve", rA), ("act", c.n["act"] if dc < 2 else tr_act[par]),
                      ("dve", c.n["dve"] if (not (do_gate and KLN >= 3) or dc < 2) else tr_dve[par])]
                t_pe = pe(lambda e, d=dc, b=bank: e.transpose(pbf(b), xnt[:, d * 128:(d + 1) * 128], identf),
                          waits=tw)
                if do_gate and KLN >= 3:
                    tr_dve[par] = dv(lambda e, b=bank, p=par: e.tensor_copy(scr2[:, p * 128:(p + 1) * 128], pbf(b)),
                                     waits=[("pe", t_pe), ("pe", gate_pe[par])])
                tr_act[par] = ac(lambda e, b=bank, col=dst_fm_col_of(dc): e.activation(
                    xn_fm[:, col:col + 128] if do_gate else xnp[:, col:col + 128], pbf(b), AF.Copy),
                    waits=[("pe", t_pe)] + ([("dve", tr_dve[par])] if (do_gate and KLN >= 3) else []))
                if do_gate and KLN >= 4:
                    gate_pe[par] = pe(lambda e, d=dc, p=par: e.matmul(
                        pb[2][:, 0:16], scr2[:, p * 128:(p + 1) * 128], gateb[:, d * 16:(d + 1) * 16],
                        start=(d == 0), stop=(d == 15), skip_group_check=True),
                        waits=[("dve", tr_dve[par])] + ([("act", gate_bank_guard[0])] if dc == 0 else []))
            if do_gate and KLN >= 5:
                gate_bank_guard[0] = ac(lambda e, t=gate_tt: e.activation(
                    gs[:, t * 16:(t + 1) * 16], pb[2][:, 0:16], AF.Sigmoid),
                    waits=[("pe", gate_pe[1])])
            return rA

        tr_act = [0, 0]
        tr_dve = [0, 0]
        gate_pe = [0, 0]
        gate_bank_guard = [0]

        out_init = []
        xbg = [[("dc", consts)], [("dc", consts)]]
        for tt in range(NT if KSUB >= 2 else 0):
            p01 = tt % 2
            def _mid(t=tt, p=p01):
                oi = dv(lambda e: e.tensor_tensor(
                    out_acc[:, t * D:(t + 1) * D], xbP[p], bout_bc, ALU.add),
                    waits=[("act", bout_ready)])
                out_init.append(oi)
            ln_tile(xo[tt * 128:(tt + 1) * 128, :], xbg[p01], True,
                    (lambda d, t=tt: d * TPC + t * 128), tt, mid_cb=_mid,
                    xb_t=xbP[p01], xnt=xntP[p01], xsem="dxx" if p01 == 0 else "dxx2")
            xbg[p01] = [("dve", c.n["dve"]), ("act", c.n["act"]), ("pe", c.n["pe"])]
        xn_main_done_act = c.n["act"]
        xb_guard = [("dve", c.n["dve"]), ("act", c.n["act"]), ("pe", c.n["pe"])]

        route = {"dw_done": 0, "scat_total": 0, "rf": [0, 0], "rt_act": 0}

        def emit_dw_routing():
          for tt in range(NT if KSUB >= 3 else 0):
              g0 = gs[:, tt * 16:(tt + 1) * 16]
              d0 = dw[:, tt * 16:(tt + 1) * 16]
              m1, m2 = lnc[:, 10:11], lnc[:, 11:12]
              i1, i2 = lnc[:, 12:13], lnc[:, 13:14]
              w1c, w2c = lnc[:, 14:15], lnc[:, 15:16]
              eq = tks[:, 0:16]
              tmp = tks[:, 16:32]
              s2 = tks[:, 32:48]
              oh1 = tks[:, 48:64]
              k0 = dv(lambda e, g=g0: e.reduce_max(m1, g, axis=AX), waits=[("act", gate_bank_guard[0])])
              k0 = dv(lambda e, g=g0: e.tensor_single_scalar(eq, g, m1, ALU.is_equal), waits=[("dve", k0)])
              k0 = dv(lambda e: e.scalar_tensor_tensor(tmp, eq, -1e6, iotaB, ALU.mult, ALU.add), waits=[("dve", k0)])
              k0 = dv(lambda e: e.tensor_reduce(i1, tmp, axis=AX, op=ALU.min), waits=[("dve", k0)])
              k0 = dv(lambda e: e.tensor_single_scalar(oh1, iota, i1, ALU.is_equal), waits=[("dve", k0)])
              k0 = dv(lambda e, g=g0: e.scalar_tensor_tensor(s2, oh1, -1e6, g, ALU.mult, ALU.add), waits=[("dve", k0)])
              k0 = dv(lambda e: e.reduce_max(m2, s2, axis=AX), waits=[("dve", k0)])
              k0 = dv(lambda e: e.tensor_single_scalar(eq, s2, m2, ALU.is_equal), waits=[("dve", k0)])
              k0 = dv(lambda e: e.scalar_tensor_tensor(tmp, eq, -1e6, iotaB, ALU.mult, ALU.add), waits=[("dve", k0)])
              k0 = dv(lambda e: e.tensor_reduce(i2, tmp, axis=AX, op=ALU.min), waits=[("dve", k0)])
              k0 = dv(lambda e: e.tensor_add(w1c, m1, m2), waits=[("dve", k0)])
              k0 = dv(lambda e: e.reciprocal(w2c, w1c), waits=[("dve", k0)])
              k0 = dv(lambda e: e.tensor_mul(w1c, m1, w2c), waits=[("dve", k0)])
              k0 = dv(lambda e: e.tensor_mul(w2c, m2, w2c), waits=[("dve", k0)])
              k0 = dv(lambda e: e.tensor_single_scalar(oh1, iota, i1, ALU.is_equal), waits=[("dve", k0)])
              k0 = dv(lambda e: e.tensor_single_scalar(eq, iota, i2, ALU.is_equal), waits=[("dve", k0)])
              k0 = dv(lambda e: e.tensor_single_scalar(s2, oh1, w1c, ALU.mult), waits=[("dve", k0)])
              k0 = dv(lambda e, d=d0: e.scalar_tensor_tensor(d, eq, w2c, s2, ALU.mult, ALU.add), waits=[("dve", k0)])
              k0 = dv(lambda e, t=tt: e.tensor_copy(i12[:, 2 * t:2 * t + 1], i1), waits=[("dve", k0)])
              k0 = dv(lambda e, t=tt: e.tensor_copy(i12[:, 2 * t + 1:2 * t + 2], i2), waits=[("dve", k0)])
              k0 = dv(lambda e, t=tt: e.tensor_copy(wcols[:, 2 * t:2 * t + 1], w1c), waits=[("dve", k0)])
              k0 = dv(lambda e, t=tt: e.tensor_copy(wcols[:, 2 * t + 1:2 * t + 2], w2c), waits=[("dve", k0)])
          route["dw_done"] = c.n["dve"]

          if KSUB >= 3:
              mprev = 0
              for tt in range(NT):
                  mprev = dv(lambda e, t=tt: e.tensor_scalar(
                      maskb[:, t * 16:(t + 1) * 16], dw[:, t * 16:(t + 1) * 16],
                      0.0, None, ALU.is_gt), waits=[("dve", route["dw_done"])])
              mask_done = mprev
              uprev = [c.n["pe"], c.n["pe"]]
              rk_stop = [0, 0]
              for T in range(NT):
                  par = T % 2
                  dus = "dua" if par == 0 else "dub"
                  dma("sync", dus, uslab[par], ubig_h[T], waits=[("pe", uprev[par])])
                  ldu = c.n[dus]
                  for half in range(2):
                      w = [(dus, ldu), ("dve", mask_done)] if half == 0 else []
                      mm = pe(lambda e, t=T, hh=half: e.matmul(
                          pb[4 + hh][0:16, 0:512], maskb[:, t * 16:(t + 1) * 16],
                          uslab[t % 2][:, hh * 512:(hh + 1) * 512],
                          start=(t == 0), stop=(t == NT - 1), skip_group_check=True),
                          waits=w, inc=True)
                      if T == NT - 1:
                          rk_stop[half] = mm
                  uprev[par] = c.n["pe"]
              rf = [0, 0]
              for half in range(2):
                  rf[half] = dv(lambda e, hh=half: e.tensor_copy(
                      rankf[:, hh * 512:(hh + 1) * 512], pb[4 + hh][0:16, 0:512]),
                      waits=[("pe", rk_stop[half])])
              rt_prev = 0
              for T in range(NT):
                  par = T % 2
                  t_pe = pe(lambda e, t=T: e.transpose(
                      pb[6 + (t % 2)][0:128, 0:16], rankf[:, t * 128:(t + 1) * 128],
                      identf[0:16, 0:16]),
                      waits=[("dve", rf[0]), ("dve", rf[1]), ("act", rt_prev)])
                  rt_prev = ac(lambda e, t=T: e.activation(
                      rank_tm[:, t * 16:(t + 1) * 16], pb[6 + (t % 2)][0:128, 0:16], AF.Copy),
                      waits=[("pe", t_pe)])
              oh = tks[:, 0:16]
              r1 = lnc[:, 10:11]
              over = lnc[:, 11:12]
              t1c = lnc[:, 12:13]
              t2c = lnc[:, 13:14]
              for T in range(NT):
                  for k2 in range(2):
                      cidx = 2 * T + k2
                      k0 = dv(lambda e, ci=cidx: e.tensor_single_scalar(
                          oh, iota, i12[:, ci:ci + 1], ALU.is_equal),
                          waits=[("act", rt_prev), ("dve", c.n["dve"])])
                      k0 = dv(lambda e, t=T: e.tensor_mul(oh, oh, rank_tm[:, t * 16:(t + 1) * 16]),
                              waits=[("dve", k0)])
                      k0 = dv(lambda e: e.reduce_sum(r1, oh, axis=AX), waits=[("dve", k0)])
                      k0 = dv(lambda e: e.tensor_scalar(over, r1, float(CCAP), None, ALU.is_gt),
                              waits=[("dve", k0)])
                      k0 = dv(lambda e, ci=cidx: e.scalar_tensor_tensor(
                          t1c, i12[:, ci:ci + 1], float(CCAP), r1, ALU.mult, ALU.add),
                          waits=[("dve", k0)])
                      k0 = dv(lambda e: e.tensor_scalar(t1c, t1c, -1.0, None, ALU.add),
                              waits=[("dve", k0)])
                      k0 = dv(lambda e: e.tensor_scalar(t2c, over, -1.0, 1.0, ALU.mult, ALU.add),
                              waits=[("dve", k0)])
                      k0 = dv(lambda e: e.tensor_mul(t1c, t1c, t2c), waits=[("dve", k0)])
                      k0 = dv(lambda e: e.scalar_tensor_tensor(
                          t1c, over, float(PADROW), t1c, ALU.mult, ALU.add), waits=[("dve", k0)])
                      k0 = dv(lambda e, ci=cidx: e.tensor_copy(idxc[:, ci:ci + 1], t1c),
                              waits=[("dve", k0)])
              idx_done = c.n["dve"]
              xnw = c.n["dxw"]
              stg = attn_g[:, 0:2048]
              for T in range(NT):
                  dma("sync", "dxb", stg, xntm_d[T * 128:(T + 1) * 128, :],
                      waits=[("dxw", xnw), ("dsc", c.n["dsc"]), ("pe", c.n["pe"])])
                  ldb = c.n["dxb"]
                  for k2 in range(2):
                      c.emit("gpsimd", lambda e, ci=2 * T + k2: e.indirect_dma_start(
                          xe_d, IndirectOffsetOnAxis(idxc[:, ci:ci + 1], 0), stg, None),
                          waits=[("dxb", ldb), ("dve", idx_done), ("dz", dz_total)],
                          inc="dsc", inc_n=16)
              route["scat_total"] = c.n["dsc"]
              route["rf"] = rf
              route["rt_act"] = rt_prev

        slab_guard = [0, 0, 0]
        GSEM = ["dga", "dgb", "dgc"]
        wqs_extra = [[("dxw", 0)]]
        wqs_first = [[]]
        proj_psum_guard = [0, 0]

        def proj(jcs, src_fm, src_stride, goff, dst_of, src_ready_act):
            war = [("pe", c.n["pe"]), ("dve", c.n["dve"])]
            evacs = []
            for si, jc in enumerate(jcs):
                par = si % 3
                ppar = si % 2
                gsem = GSEM[par]
                xw = wqs_extra[0] if par == 2 else []
                if par == 2:
                    wqs_extra[0] = []
                xw = xw + wqs_first[0]
                wqs_first[0] = []
                dma("gpsimd" if par == 1 else "sync", gsem, wqs[par], wqkv16_d[jc],
                    waits=[("pe", slab_guard[par]), ("dk", wq16_done)] + xw)
                ldt = c.n[gsem]
                for dc in range(16):
                    w = [(gsem, ldt), ("act", src_ready_act)] if dc == 0 else []
                    if dc == 0:
                        w.append(("dve", proj_psum_guard[ppar]))
                    mm = pe(lambda e, d=dc, p=par, pp=ppar: e.matmul(
                        pb[pp][:, 0:256], wqs[p][:, d * 128:(d + 1) * 128],
                        src_fm[:, d * src_stride + goff:d * src_stride + goff + 256],
                        start=(d == 0), stop=(d == 15), skip_group_check=True),
                        waits=w, inc=(dc == 15))
                slab_guard[par] = mm
                buf, col = dst_of(jc)
                ev = dv(lambda e, b=buf, cl=col, p=ppar, j=jc: e.tensor_scalar(
                    b[:, cl:cl + 256], pb[p][:, 0:256], bqkv[:, j:j + 1], None, ALU.add),
                    waits=[("pe", mm)] + war)
                proj_psum_guard[ppar] = ev
                evacs.append(ev)
            return evacs[-1]

        scB = [0, 0]
        ioB = [0, 0]
        kaB = [0, 0]
        vtB = [0, 0]
        sbfG = [0, 0]
        kdG = [0, 0]
        vtG = [0, 0]
        qdG = [0, 0]
        smG = [0, 0]
        Supd = [0] * H
        Scast = [0] * H

        def scan_chunk(cc, main, ready_dve, attn_guard_pe):
            for h in range(H):
                par = h % 2
                kcol = h * 256 + cc * 128
                mcol = h * 128
                if main:
                    Scast[h] = ac(lambda e, hh=h, p=par: e.activation(
                        sbf[p], Ssb[:, hh * 128:(hh + 1) * 128], AF.Copy),
                        waits=[("pe", sbfG[par]), ("dve", Supd[h])])
                    mm_sc = pe(lambda e, p=par, kc=kcol: e.matmul(
                        pbf(p), kb[:, kc:kc + 128], qb[:, kc:kc + 128],
                        start=True, stop=True, skip_group_check=True),
                        waits=[("dve", ready_dve), ("dve", scB[par])])
                    smG_new = dv(lambda e, p=par, mc=mcol: e.tensor_tensor(
                        sm[p], pbf(p), masks[:, mc:mc + 128], ALU.mult),
                        waits=[("pe", mm_sc), ("pe", smG[par])])
                    scB[par] = smG_new
                    qd_new = dv(lambda e, p=par, kc=kcol, mc=mcol: e.tensor_tensor(
                        qd[p], qb[:, kc:kc + 128], dqb[:, mc:mc + 128], ALU.mult),
                        waits=[("pe", qdG[par]), ("dve", ready_dve)])
                mm_kt = pe(lambda e, p=par, kc=kcol: e.transpose(
                    pbbf(4 + p), kb[:, kc:kc + 128], identb),
                    waits=[("dve", ready_dve), ("dve", kaB[par])])
                mm_vt = pe(lambda e, p=par, kc=kcol: e.transpose(
                    pbbf(6 + p), vb[:, kc:kc + 128], identb),
                    waits=[("dve", ready_dve), ("act", vtB[par])])
                kd_new = dv(lambda e, p=par, hh=h: e.tensor_scalar(
                    kdT[p], pbbf(4 + p), deck[:, hh:hh + 1], None, ALU.mult),
                    waits=[("pe", mm_kt), ("pe", kdG[par])])
                vt_new = ac(lambda e, p=par: e.activation(vT[p], pbbf(6 + p), AF.Copy),
                            waits=[("pe", mm_vt), ("pe", vtG[par])])
                vtB[par] = vt_new
                if main:
                    mm_io = pe(lambda e, p=par: e.matmul(
                        pbf(2 + p), vT[p], sm[p], start=True, stop=False, skip_group_check=True),
                        waits=[("act", vt_new), ("dve", smG_new), ("act", ioB[par])])
                    mm_io = pe(lambda e, p=par: e.matmul(
                        pbf(2 + p), sbf[p], qd[p], start=False, stop=True, skip_group_check=True),
                        waits=[("act", Scast[h]), ("dve", qd_new)])
                    sbfG[par] = mm_io
                    qdG[par] = mm_io
                    smG[par] = mm_io
                    ioB[par] = ac(lambda e, p=par, kc=kcol: e.activation(
                        attn_g[:, kc:kc + 128], pbf(2 + p), AF.Copy),
                        waits=[("pe", mm_io), ("pe", attn_guard_pe)] + attn_g_zwait[0])
                    attn_g_zwait[0] = []
                mm_A = pe(lambda e, p=par: e.matmul(
                    pb[4 + p][:, 64:192], kdT[p], vT[p], start=True, stop=True, skip_group_check=True),
                    waits=[("dve", kd_new), ("act", vt_new)])
                kdG[par] = mm_A
                vtG[par] = mm_A
                Supd[h] = dv(lambda e, hh=h, p=par: e.scalar_tensor_tensor(
                    Ssb[:, hh * 128:(hh + 1) * 128], Ssb[:, hh * 128:(hh + 1) * 128],
                    float(LAMC[hh]), pb[4 + p][:, 64:192], ALU.mult, ALU.add),
                    waits=[("pe", mm_A)] + ([("act", Scast[h])] if main else [("dve", S_zero)]))
                kaB[par] = Supd[h]

        wqs_first[0] = [("dve", c.n["dve"]), ("act", c.n["act"]), ("pe", c.n["pe"])]
        if PHASES < 2:
            emit_dw_routing()
        xnp_guard = [0]
        for pg in range(NG if PHASES >= 2 else 0):
            for t2 in range(2):
                ln_tile(xp[(pg * 2 + t2) * 128:(pg * 2 + t2 + 1) * 128, :],
                        xb_guard, False, (lambda d, tl=t2: d * 256 + tl * 128), 0)
                xb_guard = [("dve", c.n["dve"]), ("act", c.n["act"]), ("pe", c.n["pe"])]
            ev = proj(range(16, 48), xnp, 256, 0,
                      lambda jc: (kb, (jc - 16) * 256) if jc < 32 else (vb, (jc - 32) * 256),
                      c.n["act"])
            if pg == 0:
                emit_dw_routing()
                kaB[0] = max(kaB[0], route["rf"][0])
                kaB[1] = max(kaB[1], route["rf"][1])
                vtB[0] = max(vtB[0], route["rt_act"])
                vtB[1] = max(vtB[1], route["rt_act"])
                attn_g_zwait[0] = attn_g_zwait[0] + [("dsc", route["scat_total"])]
            for cc in range(2):
                scan_chunk(cc, False, ev, 0)
            xb_guard = [("pe", c.n["pe"]), ("dve", c.n["dve"]), ("act", c.n["act"])]
        Sgate = dv(lambda e: e.tensor_single_scalar(Ssb, Ssb, gatev, ALU.mult),
                   waits=[("dve", max(Supd)), ("dc", consts)])
        for h in range(H):
            Supd[h] = Sgate

        outproj_done_pe = [0]
        for g in range(NG if PHASES >= 3 else 0):
            ev = proj(range(0, 48), xn_fm, TPC, g * 256,
                      lambda jc: (qb, jc * 256) if jc < 16 else
                                 ((kb, (jc - 16) * 256) if jc < 32 else (vb, (jc - 32) * 256)),
                      xn_main_done_act)
            scB[0] = max(scB[0], proj_psum_guard[0])
            scB[1] = max(scB[1], proj_psum_guard[1])
            for cc in range(2 if KP3 >= 2 else 0):
                scan_chunk(cc, True, ev, outproj_done_pe[0])
            attn_done_act = c.n["act"]
            scan_done_dve = c.n["dve"]
            op_stop = [0] * 8
            for kc in range(16 if KP3 >= 3 else 0):
                par = kc % 3
                gsem = GSEM[par]
                dma("gpsimd" if par == 1 else "sync", gsem, wqs[par], wout16_d[kc],
                    waits=[("pe", slab_guard[par]), ("dk", wo16_done)])
                ldt = c.n[gsem]
                for tch in range(2):
                    for dcol in range(4):
                        bi = tch * 4 + dcol
                        w = []
                        if kc == 0:
                            w = [("act", attn_done_act), ("dve", scan_done_dve),
                                 ("dve", opB_guard[bi])]
                        if dcol == 0 and tch == 0:
                            w.append((gsem, ldt))
                        last_of_kc = (tch == 1 and dcol == 3)
                        mm = pe(lambda e, k2=kc, t2=tch, d2=dcol, p=par, b=bi: e.matmul(
                            pb[b][:, 0:512],
                            attn_g[:, k2 * 256 + t2 * 128:k2 * 256 + t2 * 128 + 128],
                            wqs[p][:, d2 * 512:(d2 + 1) * 512],
                            start=(k2 == 0), stop=(k2 == 15), skip_group_check=True),
                            waits=w, inc=(kc == 15 or last_of_kc))
                        if kc == 15 and mm is not None:
                            op_stop[bi] = mm
                slab_guard[par] = c.n["pe"]
            for tch in range(2 if KP3 >= 3 else 0):
                for dcol in range(4):
                    bi = tch * 4 + dcol
                    col = (g * 2 + tch) * D + dcol * 512
                    opB_guard[bi] = dv(lambda e, cl=col, b=bi: e.tensor_tensor(
                        out_acc[:, cl:cl + 512], out_acc[:, cl:cl + 512], pb[b][:, 0:512], ALU.add),
                        waits=[("pe", op_stop[bi]), ("dve", out_init[g * 2 + tch])])
            outproj_done_pe[0] = c.n["pe"]
            for i2 in range(2):
                proj_psum_guard[i2] = max(proj_psum_guard[i2], opB_guard[i2])
                scB[i2] = max(scB[i2], opB_guard[i2])
            ioB[0] = ioB[1] = c.n["act"]
            kaB[0] = max(kaB[0], opB_guard[4])
            kaB[1] = max(kaB[1], opB_guard[5])
            vtB[0] = vtB[1] = c.n["act"]

        PEa, DVa, ACa = c.n["pe"], c.n["dve"], c.n["act"]
        barrier = [("pe", PEa), ("dve", DVa), ("act", ACa)]

        ab_guard = [ACa, ACa, DVa, DVa]
        y_guard = [DVa, DVa]
        sl_guard = [0, 0]
        w13_guard = [0, 0]
        w2_guard = [0, 0]
        hbuf_guard = [PEa]

        def mlp(nchunks, w13src, w2src, jbase, use_dw, first):
            for j in range(nchunks):
                par = j % 2
                wsem = "dwa" if par == 0 else "dwb"
                wsem2 = "dwa2" if par == 0 else "dwb2"
                wts = [("pe", w13_guard[par])] + (barrier if first and j < 2 else [])
                dma("sync", wsem, w13[2 * par + 0], w13src[jbase + j][:, 0:2048], waits=wts)
                dma("gpsimd", wsem2, w13[2 * par + 1], w13src[jbase + j][:, 2048:4096], waits=wts)
                ldt = c.n[wsem]
                ldt2 = c.n[wsem2]
                for th in range(2):
                    pab = (j * 2 + th) % 2
                    stops = []
                    for which in range(2):
                        bank = pab + 2 * which
                        for dc in range(16):
                            w = []
                            if dc == 0:
                                w = [(wsem, ldt) if which == 0 else (wsem2, ldt2),
                                     (("act", ab_guard[bank]) if which == 0
                                      else ("dve", ab_guard[bank]))]
                            mm = pe(lambda e, d=dc, b=bank, p=par, wh=which, t=th: e.matmul(
                                pb[b][:, 0:512], w13[2 * p + wh][:, d * 128:(d + 1) * 128],
                                xn_fm[:, d * TPC + t * 512:d * TPC + t * 512 + 512],
                                start=(d == 0), stop=(d == 15), skip_group_check=True),
                                waits=w, inc=(dc == 15))
                            if dc == 15:
                                stops.append(mm)
                    w13_guard[par] = stops[-1]
                    sp = th
                    a_sl = ac(lambda e, s=sp, b=pab: e.activation(silu_sb[s], pb[b][:, 0:512], AF.Silu),
                              waits=[("pe", stops[0]), ("dve", sl_guard[sp])])
                    ab_guard[pab] = a_sl
                    hcol = j * TPC + th * 512
                    h1 = dv(lambda e, s=sp, b=2 + pab, cl=hcol: e.tensor_tensor(
                        hbuf[:, cl:cl + 512], silu_sb[s], pb[b][:, 0:512], ALU.mult),
                        waits=[("act", a_sl), ("pe", stops[1]), ("pe", hbuf_guard[0])])
                    ab_guard[2 + pab] = h1
                    sl_guard[sp] = h1
            h_done = c.n["dve"]
            for dcol in range(4):
                par = dcol % 2
                w2sem = "dw2a" if par == 0 else "dw2b"
                dma("gpsimd", w2sem, w2g[par], w2src[dcol],
                    waits=[("pe", w2_guard[par])] + (barrier if first and dcol < 2 else []))
                ldt = c.n[w2sem]
                for tch in range(NT):
                    yb = 4 + (tch % 2)
                    for j in range(nchunks):
                        w = []
                        if j == 0:
                            w = [(w2sem, ldt), ("dve", y_guard[tch % 2]), ("dve", h_done)]
                        mm = pe(lambda e, j2=j, b=yb, t=tch, d=dcol, p=par: e.matmul(
                            pb[b][:, 0:512], hbuf[:, j2 * TPC + t * 128:j2 * TPC + t * 128 + 128],
                            w2g[p][:, j2 * 512:(j2 + 1) * 512],
                            start=(j2 == 0), stop=(j2 == nchunks - 1), skip_group_check=True),
                            waits=w, inc=(j == nchunks - 1))
                    col = tch * D + dcol * 512
                    if use_dw is None:
                        y_guard[tch % 2] = dv(lambda e, cl=col, b=yb: e.tensor_tensor(
                            out_acc[:, cl:cl + 512], out_acc[:, cl:cl + 512], pb[b][:, 0:512], ALU.add),
                            waits=[("pe", mm)])
                    else:
                        dcol_w = use_dw(tch)
                        y_guard[tch % 2] = dv(lambda e, cl=col, b=yb, wc=dcol_w: e.scalar_tensor_tensor(
                            out_acc[:, cl:cl + 512], pb[b][:, 0:512], wc,
                            out_acc[:, cl:cl + 512], ALU.mult, ALU.add),
                            waits=[("pe", mm), ("dve", route["dw_done"])])
                w2_guard[par] = c.n["pe"]
            hbuf_guard[0] = c.n["pe"]

        if PHASES >= 4:
            mlp(11, sw13_h, sw2_h[0], 0, None, True)
            mlp(11, sw13_h, sw2_h[1], 11, None, False)

            PEs, DVs, ACs = c.n["pe"], c.n["dve"], c.n["act"]
            barrier2 = [("pe", PEs), ("dve", DVs), ("act", ACs)]
            xet_guard = [0]
            xef_guard = [0]
            trB = [ACs, ACs]
            sp_ab = [ACs, ACs, DVs, DVs]
            sp_sl = [0, 0]
            sp_y = [ACs, ACs]
            ye_guard = [0]
            def load_transpose(ex, exw):
                for sub in range(2):
                    dma("sync", "dxe", xe_tm[:, sub * 2048:(sub + 1) * 2048],
                        xe_d[ex * CCAP + sub * 128:ex * CCAP + (sub + 1) * 128, :],
                        waits=[("dsc", route["scat_total"]), ("pe", xet_guard[0])] + exw)
                ldx = c.n["dxe"]
                aev = 0
                for sub in range(2):
                    for dcj in range(16):
                        par = (sub * 16 + dcj) % 2
                        t_pe = pe(lambda e, s2=sub, d2=dcj: e.transpose(
                            pbbf(6 + ((s2 * 16 + d2) % 2)),
                            xe_tm[:, s2 * 2048 + d2 * 128:s2 * 2048 + (d2 + 1) * 128], identb),
                            waits=[("dxe", ldx), ("act", trB[par]), ("pe", xef_guard[0])])
                        aev = ac(lambda e, s2=sub, d2=dcj, p2=par: e.activation(
                            xe_fm[:, d2 * 256 + s2 * 128:d2 * 256 + s2 * 128 + 128],
                            pbbf(6 + p2), AF.Copy), waits=[("pe", t_pe)])
                        trB[par] = aev
                xet_guard[0] = c.n["pe"]
                return aev

            nxt_done = load_transpose(0, barrier2)
            for ex in range(NE):
                xef_done = nxt_done
                for j in range(NJ):
                    par = j % 2
                    wsem = "dwa" if par == 0 else "dwb"
                    wsem2 = "dwa2" if par == 0 else "dwb2"
                    wts = [("pe", w13_guard[par])]
                    dma("sync", wsem, w13[2 * par + 0], w13_h[ex][j][:, 0:2048], waits=wts)
                    dma("gpsimd", wsem2, w13[2 * par + 1], w13_h[ex][j][:, 2048:4096], waits=wts)
                    ldt = c.n[wsem]
                    ldt2 = c.n[wsem2]
                    pab = j % 2
                    stops = []
                    for which in range(2):
                        bank = pab + 2 * which
                        for dc in range(16):
                            w = []
                            if dc == 0:
                                w = [(wsem, ldt) if which == 0 else (wsem2, ldt2),
                                     ("act", xef_done),
                                     (("act", sp_ab[bank]) if which == 0
                                      else ("dve", sp_ab[bank]))]
                            mm = pe(lambda e, d=dc, b=bank, p=par, wh=which: e.matmul(
                                pb[b][:, 0:256], w13[2 * p + wh][:, d * 128:(d + 1) * 128],
                                xe_fm[:, d * 256:(d + 1) * 256],
                                start=(d == 0), stop=(d == 15), skip_group_check=True),
                                waits=w, inc=(dc == 15))
                            if dc == 15:
                                stops.append(mm)
                    w13_guard[par] = stops[-1]
                    a_sl = ac(lambda e, s2=pab, b=pab: e.activation(
                        silu_sb[s2][:, 0:256], pb[b][:, 0:256], AF.Silu),
                        waits=[("pe", stops[0]), ("dve", sp_sl[pab])])
                    sp_ab[pab] = a_sl
                    h1 = dv(lambda e, s2=pab, b=2 + pab, cl=j * 256: e.tensor_tensor(
                        he[:, cl:cl + 256], silu_sb[s2][:, 0:256], pb[b][:, 0:256], ALU.mult),
                        waits=[("act", a_sl), ("pe", stops[1])])
                    sp_ab[2 + pab] = h1
                    sp_sl[pab] = h1
                h_done = c.n["dve"]
                xef_guard[0] = c.n["pe"]
                if ex + 1 < NE:
                    nxt_done = load_transpose(ex + 1, [])
                for dcol in range(4):
                    par = dcol % 2
                    w2sem = "dw2a" if par == 0 else "dw2b"
                    dma("gpsimd", w2sem, w2g[par], w2_h[ex][dcol], waits=[("pe", w2_guard[par])])
                    ldt = c.n[w2sem]
                    for csub in range(2):
                        yb = 4 + ((dcol * 2 + csub) % 2)
                        for j in range(NJ):
                            w = []
                            if j == 0:
                                w = [(w2sem, ldt), ("act", sp_y[(dcol * 2 + csub) % 2]),
                                     ("dve", h_done)]
                            mm = pe(lambda e, j2=j, b=yb, cs=csub, d=dcol, p=par: e.matmul(
                                pb[b][:, 0:512],
                                he[:, j2 * 256 + cs * 128:j2 * 256 + cs * 128 + 128],
                                w2g[p][:, j2 * 512:(j2 + 1) * 512],
                                start=(j2 == 0), stop=(j2 == NJ - 1), skip_group_check=True),
                                waits=w, inc=(j == NJ - 1))
                        sp_y[(dcol * 2 + csub) % 2] = ac(
                            lambda e, cs=csub, d=dcol, b=yb: e.activation(
                                ye_sb[:, cs * 2048 + d * 512:cs * 2048 + d * 512 + 512],
                                pb[b][:, 0:512], AF.Copy),
                            waits=[("pe", mm), ("dye", ye_guard[0]), ("dz", dz_total)])
                    w2_guard[par] = c.n["pe"]
                ye_done_act = c.n["act"]
                for csub in range(2):
                    dma("gpsimd", "dye", ye_d[ex * CCAP + csub * 128:ex * CCAP + (csub + 1) * 128, :],
                        ye_sb[:, csub * 2048:(csub + 1) * 2048], waits=[("act", ye_done_act)])
                ye_guard[0] = c.n["dye"]

            ye_total = c.n["dye"]
            PEc, DVc, ACc = c.n["pe"], c.n["dve"], c.n["act"]
            barrier3 = [("pe", PEc), ("dve", DVc), ("act", ACc)]
            comb_guard = [0]
            for T in range(NT):
                gt = []
                for k2 in range(2):
                    g_ = c.emit("gpsimd", lambda e, ci=2 * T + k2, kk=k2: e.indirect_dma_start(
                        ybuf[kk], None, ye_d, IndirectOffsetOnAxis(idxc[:, ci:ci + 1], 0)),
                        waits=[("dye", ye_total), ("dve", comb_guard[0])]
                              + (barrier3 if T == 0 else []),
                        inc="dgt", inc_n=16)
                    gt.append(g_)
                for k2 in range(2):
                    comb_guard[0] = dv(lambda e, t=T, ci=2 * T + k2, kk=k2: e.scalar_tensor_tensor(
                        out_acc[:, t * D:(t + 1) * D], ybuf[kk], wcols[:, ci:ci + 1],
                        out_acc[:, t * D:(t + 1) * D], ALU.mult, ALU.add),
                        waits=[("dgt", gt[1]), ("dve", c.n["dve"])])

        total_dve = c.n["dve"]
        for tt in range(NT):
            dst = y_h[tt * 128:(tt + 1) * 128, :]
            c.emit("sync", lambda e, d=dst, t=tt: e.dma_start(d, out_acc[:, t * D:(t + 1) * D]),
                   waits=[("dve", total_dve)], inc="do", inc_n=16)
        c.emit("sync", lambda e: e.wait_ge(sems["do"], 16 * NT), waits=[])

    opB_guard = [0] * 8

    from contextlib import ExitStack
    with ExitStack() as st:
        for s_name in SEMS:
            sems[s_name] = st.enter_context(nc.semaphore(s_name))
        with nc.Block() as blk:
            final = {}

            def run(kind, eng):
                cc = S(kind, eng, sems)
                for i in range(8):
                    opB_guard[i] = 0
                sched(cc)
                final[kind] = dict(cc.n)

            blk.sync(lambda e: run("sync", e))
            blk.gpsimd(lambda e: run("gpsimd", e))
            blk.tensor(lambda e: run("tensor", e))
            blk.vector(lambda e: run("vector", e))
            blk.scalar(lambda e: run("scalar", e))
            kinds = list(final)
            for k in kinds[1:]:
                assert final[k] == final[kinds[0]], "schedule replay diverged"
    return nc


_NC_CACHE = None


def kernel(L0_attn_x, L0_attn_w_qkv, L0_attn_b_qkv, L0_attn_w_out, L0_attn_b_out,
           L0_ffn_gate_w, L0_ffn_w1, L0_ffn_w2, L0_ffn_w3,
           L0_ffn_sw1, L0_ffn_sw2, L0_ffn_sw3):
    global _NC_CACHE
    X = np.ascontiguousarray(np.asarray(L0_attn_x, np.float32))
    imgs = _images(L0_attn_w_qkv, L0_attn_b_qkv, L0_attn_w_out, L0_attn_b_out,
                   L0_ffn_gate_w, L0_ffn_w1, L0_ffn_w2, L0_ffn_w3,
                   L0_ffn_sw1, L0_ffn_sw2, L0_ffn_sw3)
    consts = _host_consts()
    shared = {**imgs, **consts}
    in_maps = []
    for cidx in range(8):
        b, half = cidx // 2, cidx % 2
        xo = np.ascontiguousarray(X[b, half * 1024:(half + 1) * 1024, :])
        xpre = np.ascontiguousarray(X[b, 0:1024, :]) if half else xo
        gv = np.full((128, 1), float(half), np.float32)
        in_maps.append({"xo": xo, "xp": xpre, "gatev": gv, **shared})
    if _NC_CACHE is None:
        _NC_CACHE = build_nc()
    res = run_bass_kernel_spmd(_NC_CACHE, in_maps, core_ids=list(range(8)))
    out = np.empty((4, 2048, 2048), np.float32)
    for cidx in range(8):
        b, half = cidx // 2, cidx % 2
        out[b, half * 1024:(half + 1) * 1024, :] = res.results[cidx]["y"]
    return out
